# version 15
# baseline (speedup 1.0000x reference)
import math
import contextlib
import numpy as np
import ml_dtypes
import concourse.bass as bass
import concourse.mybir as mybir
from concourse.bass_utils import run_bass_kernel_spmd

F32 = mybir.dt.float32
BF16 = mybir.dt.bfloat16
AF = mybir.ActivationFunctionType
ALU = mybir.AluOpType
AX = mybir.AxisListType

B, S, E, H, D, FF = 4, 4096, 2048, 16, 128, 8192
NT = S // 128
NQ = NT // 2
TOWN = NQ * 128
SCALE = 1.0 / math.sqrt(D)
SQD = math.sqrt(D)
SHIFT_C = 10.0
EPS = 1e-5
_SLOPES = [2.0 ** (-0.5 * (i + 1)) for i in range(H)]
ALIBI_WIN = 45.0
_WT = [min(NT, math.ceil((ALIBI_WIN / sl + 127.0) / 128) + 1) for sl in _SLOPES]
_DBG_TAPS = ()
_DBG_P1 = False
_DBG_P2 = False
_LAST_RES = []


class Prog:
    def __init__(self):
        self.items = []
        self.cnt = {}

    def ev(self, sem):
        return self.cnt.get(sem, 0)

    def op(self, eng, fn, waits=(), incs=()):
        waits = [(s, c) for (s, c) in waits if c and c > 0]
        for s, c in waits:
            assert self.cnt.get(s, 0) >= c, (eng, s, c, self.cnt.get(s, 0))
        for s, a in incs:
            self.cnt[s] = self.cnt.get(s, 0) + a
        self.items.append((eng, tuple(waits), fn, tuple(incs)))
        return self.cnt[incs[0][0]] if incs else None

    def dma(self, eng, out, in_, sem, waits=(), transpose=False, slow=False):
        def fn(e, out=out, in_=in_):
            if transpose:
                return e.dma_start_transpose(out, in_)
            if slow:
                return e.dma_start(out, in_, allow_slow_non_contiguous=True)
            return e.dma_start(out, in_)
        return self.op(eng, fn, waits=waits, incs=((sem, 16),))


SEMS = ["s_g", "s_gi", "s_pr1", "s_pw1", "s_const", "s_x0", "s_x1", "s_wres", "s_pr", "s_pw",
        "s_ksp0", "s_ksp1", "s_vsp0", "s_vsp1", "s_c2", "s_h0", "s_h1",
        "s_at0", "s_at1", "s_aT0", "s_aT1", "s_rs0", "s_rs1", "s_wo",
        "s_wg0", "s_wg1", "s_wu0", "s_wu1", "s_wd0", "s_wd1", "s_x1s",
        "s_out", "s_dbg", "s_pe", "s_act", "s_dve"]


def build_nc():
    nc = bass.Bass("TRN2", target_bir_lowering=False)
    p = Prog()

    T = {}
    def dram(name, shape, dt, kind=None):
        t = nc.dram_tensor(name, shape, dt, kind=kind) if kind else \
            nc.dram_tensor(name, shape, dt)
        T[name] = t
        return t

    for n, sh in (("xb", [S, E]), ("xq", [TOWN, E]), ("wq", [E, E]),
                  ("wk", [E, E]), ("wv", [E, E]), ("wo", [E, E]),
                  ("wg", [FF, E]), ("wu", [FF, E]), ("wd", [E, FF]),
                  ("posq", [1, TOWN]), ("posk", [128, NT])):
        dram(n, sh, F32, "ExternalInput")
    dram("maskd", [2, 128, 128], BF16, "ExternalInput")
    for n in ("bq", "bk", "bv", "bo"):
        dram(n, [E], F32, "ExternalInput")
    dram("eye", [128, 128], BF16, "ExternalInput")
    dram("out", [TOWN, E], F32, "ExternalOutput")

    for n, sh in (("wkb", [E, E]), ("wvb", [E, E]), ("wqb", [E, E]),
                  ("wob", [E, E]), ("wgb", [FF, E]), ("wub", [FF, E]),
                  ("wdb", [E, FF]),
                  ("woTd", [16, 128, E]), ("wgTd", [16, 128, FF]),
                  ("wuTd", [16, 128, FF]), ("wdTd", [64, 128, E]),
                  ("kts", [E, S]), ("qts", [E, TOWN]),
                  ("vts", [NT, H, 128, 128]), ("ats", [TOWN, E])):
        dram(n, sh, BF16)
    dram("x1s", [TOWN, E], F32)

    ctx = contextlib.ExitStack()
    with ctx:
        e_ = ctx.enter_context
        PS = [e_(nc.psum_tensor(f"ps{i}", [128, 512], F32)) for i in range(6)]
        PT = [e_(nc.psum_tensor("pt6", [128, 1024], BF16)),
              e_(nc.psum_tensor("pt7", [128, 1024], BF16))]
        sb = {}
        for n, sh, dt in (("eye_sb", [128, 128], BF16),
                          ("ones_b", [1, 128], BF16),
                          ("ones_f", [1, 128], F32),
                          ("bk_sb", [128, 16], F32),
                          ("bq_sb", [128, 16], F32),
                          ("bvrow", [1, E], BF16),
                          ("borow", [1, E], BF16),
                          ("stats", [128, 64], F32),
                          ("sqs", [128, 2, E], BF16)):
            sb[n] = e_(nc.sbuf_tensor(n, sh, dt))
        sems = {n: e_(nc.semaphore(n)) for n in SEMS}

        _emit(nc, p, T, PS, PT, sb)

        with nc.Block() as block:
            for name, deco in (("sync", block.sync), ("gpsimd", block.gpsimd),
                               ("tensor", block.tensor), ("scalar", block.scalar),
                               ("vector", block.vector)):
                def runner(e, name=name):
                    for (eng, waits, fn, incs) in p.items:
                        if eng != name:
                            continue
                        for (s, c) in waits:
                            e.wait_ge(sems[s], c)
                        ins = fn(e)
                        for (s, a) in incs:
                            ins.then_inc(sems[s], a)
                deco(runner)
    return nc


def _emit(nc, p, T, PS, PT, sb):
    eye_sb, ones_b, ones_f = sb["eye_sb"], sb["ones_b"], sb["ones_f"]
    bk_sb, bq_sb = sb["bk_sb"], sb["bq_sb"]
    bvrow, borow = sb["bvrow"], sb["borow"]
    stats, sqs = sb["stats"], sb["sqs"]
    ap = lambda n: T[n].ap()

    p.dma("sync", eye_sb[:, :], ap("eye"), "s_const")
    p.dma("sync", bk_sb[:, :], bass.AP(T["bk"], 0, [[1, 128], [128, 16]]),
          "s_const", slow=True)
    p.dma("sync", bq_sb[:, :], bass.AP(T["bq"], 0, [[1, 128], [128, 16]]),
          "s_const", slow=True)
    c_bvrow = p.dma("gpsimd", sb["bvrow"][:, :], ap("bv"), "s_const")
    c_borow = p.dma("gpsimd", sb["borow"][:, :], ap("bo"), "s_const")
    c_setup = p.ev("s_const")
    p.op("vector", lambda e: e.memset(ones_b[:, :], 1.0), incs=(("s_dve", 1),))
    p.op("vector", lambda e: e.memset(ones_f[:, :], 1.0), incs=(("s_dve", 1),))
    c_ones = p.ev("s_dve")

    cast_done = {}
    prev_cast = [0]
    def cast(name, chunks):
        src, dst = ap(name), T[name + "b"]
        rows = src.shape[0]
        per = rows // chunks
        for i in range(chunks):
            w = (("s_g", prev_cast[0]),) if (i == 0 and prev_cast[0]) else ()
            p.dma("gpsimd", dst.ap()[i * per:(i + 1) * per, :],
                  src[i * per:(i + 1) * per, :], "s_g", waits=w)
        cast_done[name] = prev_cast[0] = p.ev("s_g")
    for n, ch in (("wk", 2), ("wv", 2), ("wq", 2), ("wo", 2),
                  ("wg", 4), ("wu", 4), ("wd", 4)):
        cast(n, ch)

    prep_items = ([("wo", i, 0, 2048) for i in range(16)]
                  + [("wg", i, q, 2048) for i in range(16) for q in range(4)]
                  + [("wu", i, q, 2048) for i in range(16) for q in range(4)]
                  + [("wd", i, 0, 2048) for i in range(64)])
    prep_state = {"i": 0, "w": [0, 0]}

    def drain_prep(n, trw):
        meta = {"wo": (T["wob"], T["woTd"]), "wg": (T["wgb"], T["wgTd"]),
                "wu": (T["wub"], T["wuTd"]), "wd": (T["wdb"], T["wdTd"])}
        while n > 0 and prep_state["i"] < len(prep_items):
            w, idx, q, rows = prep_items[prep_state["i"]]
            prep_state["i"] += 1
            n -= 1
            srcb, dstT = meta[w]
            buf = prep_state["i"] % 2
            spr = "s_pr" if buf == 0 else "s_pr1"
            spw = "s_pw" if buf == 0 else "s_pw1"
            wts = [("s_g", cast_done[w])]
            if prep_state["w"][buf]:
                wts.append((spw, prep_state["w"][buf]))
            c1 = p.dma("sync", trw[:, buf, 0:rows],
                       srcb.ap()[q * rows:(q + 1) * rows,
                                 idx * 128:(idx + 1) * 128],
                       spr, waits=tuple(wts), transpose=True)
            prep_state["w"][buf] = p.dma(
                "sync", dstT.ap()[idx, :, q * rows:(q + 1) * rows],
                trw[:, buf, 0:rows], spw, waits=((spr, c1),))

    ln_state = {"sqs0": 0, "sqs1": 0}

    def ln(src_ap, dst_ap, sl, wait_src, extra_norm_waits=(), sq=0):
        k = f"sqs{sq}"
        wsq = wait_src + ((("s_dve", ln_state[k]),) if ln_state[k] else ())
        ca = p.op("scalar",
                  lambda e: e.activation(sqs[:, sq, :], src_ap, AF.Square),
                  waits=wsq, incs=(("s_act", 1),))
        def reduces(e):
            e.reduce_sum(stats[:, 0 + sl:1 + sl], src_ap, axis=AX.X)
            return e.reduce_sum(stats[:, 2 + sl:3 + sl], sqs[:, sq, :],
                                axis=AX.X)
        cd = p.op("vector", reduces, waits=wait_src + (("s_act", ca),),
                  incs=(("s_dve", 1),))
        ln_state[k] = cd
        cd1 = p.op("vector",
                   lambda e: (e.tensor_scalar(stats[:, 4 + sl:5 + sl],
                                              stats[:, 0 + sl:1 + sl],
                                              1.0 / E, None, ALU.mult),
                              e.tensor_scalar(stats[:, 6 + sl:7 + sl],
                                              stats[:, 2 + sl:3 + sl],
                                              1.0 / E, None, ALU.mult))[-1],
                   waits=(("s_dve", cd),), incs=(("s_dve", 1),))
        cd2 = p.op("vector",
                   lambda e: e.scalar_tensor_tensor(
                       stats[:, 8 + sl:9 + sl], stats[:, 4 + sl:5 + sl], -1.0,
                       stats[:, 4 + sl:5 + sl], ALU.mult, ALU.mult),
                   waits=(("s_dve", cd1),), incs=(("s_dve", 1),))
        cd3 = p.op("vector",
                   lambda e: e.scalar_tensor_tensor(
                       stats[:, 10 + sl:11 + sl], stats[:, 6 + sl:7 + sl], EPS,
                       stats[:, 8 + sl:9 + sl], ALU.add, ALU.add),
                   waits=(("s_dve", cd2),), incs=(("s_dve", 1),))
        ca1 = p.op("scalar",
                   lambda e: e.activation(stats[:, 12 + sl:13 + sl],
                                          stats[:, 10 + sl:11 + sl], AF.Ln),
                   waits=(("s_dve", cd3),), incs=(("s_act", 1),))
        cr = p.op("scalar",
                  lambda e: e.activation(stats[:, 14 + sl:15 + sl],
                                         stats[:, 12 + sl:13 + sl], AF.Exp,
                                         scale=-0.5),
                  waits=(("s_act", ca1),), incs=(("s_act", 1),))
        cn = p.op("vector",
                  lambda e: e.scalar_tensor_tensor(
                      stats[:, 16 + sl:17 + sl], stats[:, 4 + sl:5 + sl], -1.0,
                      stats[:, 14 + sl:15 + sl], ALU.mult, ALU.mult),
                  waits=(("s_act", cr),), incs=(("s_dve", 1),))
        co = p.op("scalar",
                  lambda e: e.activation(dst_ap, src_ap, AF.Identity,
                                         bias=stats[:, 16 + sl:17 + sl],
                                         scale=stats[:, 14 + sl:15 + sl]),
                  waits=(("s_dve", cn), ("s_act", cr)) + tuple(extra_norm_waits),
                  incs=(("s_act", 1),))
        return co

    def ln_pair(jobs):
        ca = []
        for (src, dst, sl, sq, w, ex) in jobs:
            k = f"sqs{sq}"
            wsq = tuple(w) + ((("s_dve", ln_state[k]),) if ln_state[k] else ())
            ca.append(p.op("scalar",
                           lambda e, src=src, sq=sq: e.activation(
                               sqs[:, sq, :], src, AF.Square),
                           waits=wsq, incs=(("s_act", 1),)))
        cd = []
        for i, (src, dst, sl, sq, w, ex) in enumerate(jobs):
            def reduces(e, src=src, sl=sl, sq=sq):
                e.reduce_sum(stats[:, 0 + sl:1 + sl], src, axis=AX.X)
                return e.reduce_sum(stats[:, 2 + sl:3 + sl], sqs[:, sq, :],
                                    axis=AX.X)
            c = p.op("vector", reduces, waits=tuple(w) + (("s_act", ca[i]),),
                     incs=(("s_dve", 1),))
            ln_state[f"sqs{sq}"] = c
            cd.append(c)
        cd1 = []
        for i, (src, dst, sl, sq, w, ex) in enumerate(jobs):
            def d1(e, sl=sl):
                e.tensor_scalar(stats[:, 4 + sl:5 + sl], stats[:, 0 + sl:1 + sl],
                                1.0 / E, None, ALU.mult)
                return e.tensor_scalar(stats[:, 6 + sl:7 + sl],
                                       stats[:, 2 + sl:3 + sl],
                                       1.0 / E, None, ALU.mult)
            cd1.append(p.op("vector", d1, waits=(("s_dve", cd[i]),),
                            incs=(("s_dve", 1),)))
        cd2 = []
        for i, (src, dst, sl, sq, w, ex) in enumerate(jobs):
            cd2.append(p.op("vector",
                            lambda e, sl=sl: e.scalar_tensor_tensor(
                                stats[:, 8 + sl:9 + sl], stats[:, 4 + sl:5 + sl],
                                -1.0, stats[:, 4 + sl:5 + sl],
                                ALU.mult, ALU.mult),
                            waits=(("s_dve", cd1[i]),), incs=(("s_dve", 1),)))
        cd3 = []
        for i, (src, dst, sl, sq, w, ex) in enumerate(jobs):
            cd3.append(p.op("vector",
                            lambda e, sl=sl: e.scalar_tensor_tensor(
                                stats[:, 10 + sl:11 + sl],
                                stats[:, 6 + sl:7 + sl], EPS,
                                stats[:, 8 + sl:9 + sl], ALU.add, ALU.add),
                            waits=(("s_dve", cd2[i]),), incs=(("s_dve", 1),)))
        ca1 = []
        for i, (src, dst, sl, sq, w, ex) in enumerate(jobs):
            ca1.append(p.op("scalar",
                            lambda e, sl=sl: e.activation(
                                stats[:, 12 + sl:13 + sl],
                                stats[:, 10 + sl:11 + sl], AF.Ln),
                            waits=(("s_dve", cd3[i]),), incs=(("s_act", 1),)))
        cr = []
        for i, (src, dst, sl, sq, w, ex) in enumerate(jobs):
            cr.append(p.op("scalar",
                           lambda e, sl=sl: e.activation(
                               stats[:, 14 + sl:15 + sl],
                               stats[:, 12 + sl:13 + sl], AF.Exp, scale=-0.5),
                           waits=(("s_act", ca1[i]),), incs=(("s_act", 1),)))
        cn = []
        for i, (src, dst, sl, sq, w, ex) in enumerate(jobs):
            cn.append(p.op("vector",
                           lambda e, sl=sl: e.scalar_tensor_tensor(
                               stats[:, 16 + sl:17 + sl],
                               stats[:, 4 + sl:5 + sl], -1.0,
                               stats[:, 14 + sl:15 + sl], ALU.mult, ALU.mult),
                           waits=(("s_act", cr[i]),), incs=(("s_dve", 1),)))
        co = []
        for i, (src, dst, sl, sq, w, ex) in enumerate(jobs):
            co.append(p.op("scalar",
                           lambda e, src=src, dst=dst, sl=sl: e.activation(
                               dst, src, AF.Identity,
                               bias=stats[:, 16 + sl:17 + sl],
                               scale=stats[:, 14 + sl:15 + sl]),
                           waits=(("s_dve", cn[i]),) + tuple(ex),
                           incs=(("s_act", 1),)))
        return co

    tr_war = [0, 0]

    def transpose_tile(xn_ap, xnT, ti, wait_act):
        last = 0
        pe_last = 0
        for half in range(2):
            def pe_fn(e, half=half):
                for j in range(8):
                    ins = e.transpose(PT[half][:, j * 128:(j + 1) * 128],
                                      xn_ap[:, (half * 8 + j) * 128:
                                            (half * 8 + j + 1) * 128],
                                      eye_sb[:, :])
                return ins
            wts = [("s_act", wait_act), ("s_const", c_setup)]
            if tr_war[half]:
                wts.append(("s_dve", tr_war[half]))
            ct = p.op("tensor", pe_fn, waits=tuple(wts), incs=(("s_pe", 1),))
            cc = p.op("vector",
                      lambda e, half=half, ti=ti: e.tensor_copy(
                          xnT[:, half * 8:half * 8 + 8, ti * 128:(ti + 1) * 128],
                          PT[half][:, 0:1024]),
                      waits=(("s_pe", ct),), incs=(("s_dve", 1),))
            tr_war[half] = cc
            last = cc
            pe_last = ct
        return pe_last, last

    ctx1 = contextlib.ExitStack()
    with ctx1:
        e_ = ctx1.enter_context
        xt = e_(nc.sbuf_tensor("xt", [128, 2, E], F32))
        xn1 = e_(nc.sbuf_tensor("xn1", [128, 4, E], BF16))
        xn1T = e_(nc.sbuf_tensor("xn1T", [128, 16, 512], BF16))
        ksb = e_(nc.sbuf_tensor("ksb", [128, 2, 512], BF16))
        vsb = e_(nc.sbuf_tensor("vsb", [128, 1, E], BF16))
        trw = e_(nc.sbuf_tensor("trw", [128, 2, 2048], BF16))

        st = dict(norm=[0, 0], kevac=[0, 0], kst=[0, 0], vevac=[0, 0],
                  vst=[0, 0], trpe=[0, 0, 0, 0])
        if _DBG_P1:
            dbg_xt = nc.dram_tensor("dbg_xt", [8, 128, E], F32,
                                    kind="ExternalOutput")
            dbg_xn1 = nc.dram_tensor("dbg_xn1", [8, 128, E], BF16,
                                     kind="ExternalOutput")
            dbg_sts = nc.dram_tensor("dbg_sts", [8, 128, 40], F32,
                                     kind="ExternalOutput")

        def p1_loads_ln(src_ap, grp):
            norms = []
            for ti in range(4):
                tt = grp * 4 + ti
                sl = tt % 2
                sx = f"s_x{sl}"
                cx = p.dma("gpsimd", xt[:, sl, :],
                           src_ap[tt * 128:(tt + 1) * 128, :], sx,
                           waits=(("s_act", st["norm"][sl]),))
                extra = [("s_pe", st["trpe"][ti])] if st["trpe"][ti] else []
                cn = ln(xt[:, sl, :], xn1[:, ti, :], sl, ((sx, cx),),
                        extra_norm_waits=extra, sq=sl)
                st["norm"][sl] = cn
                norms.append(cn)
            return norms

        def p1_transposes(norms, xnT):
            last = 0
            for ti in range(4):
                cpe, cdv = transpose_tile(xn1[:, ti, :], xnT, ti, norms[ti])
                st["trpe"][ti] = cpe
                last = max(last, cdv)
            return last

        def proj_feature(wT_sb, bias_col, dst_dram, grp, c_ready, c_w):
            for fc in range(16):
                bsl = fc % 2
                ssp = f"s_ksp{bsl}"
                wts = [("s_dve", c_ready), ("s_wres", c_w)]
                if st["kevac"][bsl]:
                    wts.append(("s_act", st["kevac"][bsl]))
                def pe_fn(e, fc=fc, bsl=bsl, wT_sb=wT_sb):
                    for ec in range(16):
                        ins = e.matmul(PS[bsl][:, :],
                                       wT_sb[:, ec, fc * 128:(fc + 1) * 128],
                                       xn1T[:, ec, :],
                                       start=(ec == 0), stop=(ec == 15))
                    return ins
                cm = p.op("tensor", pe_fn, waits=tuple(wts), incs=(("s_pe", 1),))
                wts2 = [("s_pe", cm)]
                if st["kst"][bsl]:
                    wts2.append((ssp, st["kst"][bsl]))
                ce = p.op("scalar",
                          lambda e, fc=fc, bsl=bsl, bc=bias_col: e.activation(
                              ksb[:, bsl, :], PS[bsl][:, :], AF.Identity,
                              bias=bc[:, fc:fc + 1]),
                          waits=tuple(wts2), incs=(("s_act", 1),))
                st["kevac"][bsl] = ce
                cs = p.dma("sync",
                           dst_dram.ap()[fc * 128:(fc + 1) * 128,
                                         grp * 512:(grp + 1) * 512],
                           ksb[:, bsl, :], ssp, waits=(("s_act", ce),))
                st["kst"][bsl] = cs

        def proj_v(wvT, grp, c_ready, c_w):
            for ti in range(4):
                tt = grp * 4 + ti
                vsl = 0
                ssp = "s_vsp0"
                wts = [("s_dve", c_ready), ("s_wres", c_w),
                       ("s_const", c_setup)]
                vmax = max(st["vevac"])
                if vmax:
                    wts.append(("s_act", vmax))
                def pe_fn(e, ti=ti, wvT=wvT):
                    for fg in range(4):
                        for ec in range(16):
                            e.matmul(PS[2 + fg][:, :],
                                     xn1T[:, ec, ti * 128:(ti + 1) * 128],
                                     wvT[:, ec, fg * 512:(fg + 1) * 512],
                                     start=(ec == 0), stop=False)
                        ins = e.matmul(PS[2 + fg][:, :], ones_b[:, :],
                                       bvrow[:, fg * 512:(fg + 1) * 512],
                                       start=False, stop=True)
                    return ins
                cm = p.op("tensor", pe_fn, waits=tuple(wts), incs=(("s_pe", 1),))
                wts2 = [("s_pe", cm)]
                if st["vst"][vsl]:
                    wts2.append((ssp, st["vst"][vsl]))
                def act_fn(e, vsl=vsl):
                    for fg in range(4):
                        ins = e.activation(vsb[:, vsl, fg * 512:(fg + 1) * 512],
                                           PS[2 + fg][:, :], AF.Identity)
                    return ins
                ce = p.op("scalar", act_fn, waits=tuple(wts2), incs=(("s_act", 1),))
                st["vevac"][vsl] = ce
                vout = bass.AP(T["vts"], tt * H * 128 * 128,
                               [[128, 128], [128 * 128, H], [1, 128]])
                cs = p.dma("sync", vout, vsb[:, vsl, :], ssp,
                           waits=(("s_act", ce),))
                st["vst"][vsl] = cs

        with nc.sbuf_tensor("wkT", [128, 16, E], BF16) as wkT, \
             nc.sbuf_tensor("wvT", [128, 16, E], BF16) as wvT:
            for ec in range(16):
                p.dma("sync", wkT[:, ec, :],
                      T["wkb"].ap()[0:E, ec * 128:(ec + 1) * 128], "s_wres",
                      waits=(("s_g", cast_done["wk"]),), transpose=True)
            for ec in range(16):
                p.dma("sync", wvT[:, ec, :],
                      T["wvb"].ap()[0:E, ec * 128:(ec + 1) * 128], "s_wres",
                      waits=(("s_g", cast_done["wv"]),), transpose=True)
            c_wkv = p.ev("s_wres")
            pend = p1_loads_ln(ap("xb"), 0)
            for grp in range(8):
                c_ready = p1_transposes(pend, xn1T)
                if grp < 7:
                    pend = p1_loads_ln(ap("xb"), grp + 1)
                proj_feature(wkT, bk_sb, T["kts"], grp, c_ready, c_wkv)
                proj_v(wvT, grp, c_ready, c_wkv)
                drain_prep(7, trw)

        with nc.sbuf_tensor("wqT", [128, 16, E], BF16) as wqT:
            wts = (("s_g", cast_done["wq"]), ("s_pe", p.ev("s_pe")),
                   ("s_act", p.ev("s_act")), ("s_dve", p.ev("s_dve")))
            for ec in range(16):
                p.dma("sync", wqT[:, ec, :],
                      T["wqb"].ap()[0:E, ec * 128:(ec + 1) * 128], "s_wres",
                      waits=wts if ec == 0 else (), transpose=True)
            c_wq = p.ev("s_wres")
            pend = p1_loads_ln(ap("xq"), 0)
            for grp in range(4):
                c_ready = p1_transposes(pend, xn1T)
                if grp < 3:
                    pend = p1_loads_ln(ap("xq"), grp + 1)
                proj_feature(wqT, bq_sb, T["qts"], grp, c_ready, c_wq)
                drain_prep(8, trw)
    c_p1 = [("s_ksp0", p.ev("s_ksp0")), ("s_ksp1", p.ev("s_ksp1")),
            ("s_vsp0", p.ev("s_vsp0")), ("s_vsp1", p.ev("s_vsp1")),
            ("s_pr", p.ev("s_pr")), ("s_pr1", p.ev("s_pr1")),
            ("s_pw", p.ev("s_pw")), ("s_pw1", p.ev("s_pw1"))]


    ctx2 = contextlib.ExitStack()
    with ctx2:
        e_ = ctx2.enter_context
        khT = e_(nc.sbuf_tensor("khT", [128, 2, S], BF16))
        qhT = e_(nc.sbuf_tensor("qhT", [128, 2, TOWN], BF16))
        vhE = e_(nc.sbuf_tensor("vhE", [128, 2, NT, 132], BF16))
        ptb = e_(nc.sbuf_tensor("ptb", [128, 2, 512], BF16))
        atb = e_(nc.sbuf_tensor("atb", [128, 2, 4, 128], BF16))
        posq_sb = e_(nc.sbuf_tensor("posq_sb", [1, TOWN], F32))
        maskd_sb = e_(nc.sbuf_tensor("maskd_sb", [128, 2, 128], BF16))
        kpos = e_(nc.sbuf_tensor("kpos", [128, NT], F32))
        alib = e_(nc.sbuf_tensor("alib", [128, 2, NT], F32))
        shiftb = e_(nc.sbuf_tensor("shiftb", [1, 2, TOWN], BF16))
        rd_sb = e_(nc.sbuf_tensor("rd_sb", [128, 8], F32))
        trw2 = e_(nc.sbuf_tensor("trw2", [128, 2, 2048], BF16))

        p.dma("sync", kpos[:, :], ap("posk"), "s_c2", waits=tuple(c_p1))
        p.dma("sync", posq_sb[:, :], ap("posq"), "s_c2")
        p.dma("sync", maskd_sb[:, :, :],
              bass.AP(T["maskd"], 0, [[128, 128], [128 * 128, 2], [1, 128]]),
              "s_c2")
        c_p2setup = p.ev("s_c2")

        cms = 0
        for bufi in range(2):
            cms = p.op("vector",
                       lambda e, bufi=bufi: e.memset(vhE[:, bufi, :, 128:129],
                                                     1.0),
                       waits=tuple(c_p1), incs=(("s_dve", 1),))

        dbgT = {}
        if _DBG_P2:
            dbgT["st"] = nc.dram_tensor("dbg_st", [2, 128, 512], F32,
                                        kind="ExternalOutput")
            dbgT["sh"] = nc.dram_tensor("dbg_sh", [1, 2 * TOWN], BF16,
                                        kind="ExternalOutput")
            dbgT["pq"] = nc.dram_tensor("dbg_pq", [1, TOWN], F32,
                                        kind="ExternalOutput")
            dbgT["on"] = nc.dram_tensor("dbg_on", [1, 128], BF16,
                                        kind="ExternalOutput")
            dbgT["stsb"] = e_(nc.sbuf_tensor("dbg_stsb", [128, 2, 512], F32))
            dbgT["pv"] = nc.dram_tensor("dbg_pv", [4, 128, 132], F32,
                                        kind="ExternalOutput")
            dbgT["rd"] = nc.dram_tensor("dbg_rd", [128, 8], F32,
                                        kind="ExternalOutput")
            dbgT["pt"] = nc.dram_tensor("dbg_pt", [2, 128, 512], BF16,
                                        kind="ExternalOutput")
            dbgT["pvsb"] = e_(nc.sbuf_tensor("dbg_pvsb", [128, 4, 132], F32))
        head_pe = [0, 0]
        st_exp = [0, 0]
        pt_pv = [0, 0]
        pv_dve = 0
        at_st = [0, 0]
        al_war = [0, 0]

        for h in range(H):
            drain_prep(10, trw2)
            hb = h % 2
            sh_ = f"s_h{hb}"
            slope = _SLOPES[h]
            wld = list(c_p1) + [("s_dve", cms)]
            if head_pe[hb]:
                wld.append(("s_pe", head_pe[hb]))
            p.dma("sync", khT[:, hb, :], T["kts"].ap()[h * 128:(h + 1) * 128, :],
                  sh_, waits=tuple(wld))
            p.dma("sync", qhT[:, hb, :], T["qts"].ap()[h * 128:(h + 1) * 128, :],
                  sh_)
            vin = bass.AP(T["vts"], h * 128 * 128,
                          [[128, 128], [H * 128 * 128, NT], [1, 128]])
            p.dma("sync", vhE[:, hb, :, 0:128], vin, sh_)
            c_hload = p.ev(sh_)
            wts = [("s_c2", c_p2setup)]
            if al_war[hb]:
                wts.append(("s_act", al_war[hb]))
            if head_pe[hb]:
                wts.append(("s_pe", head_pe[hb]))
            ca1 = p.op("vector",
                       lambda e, hb=hb, slope=slope: e.tensor_scalar_mul(
                           alib[:, hb, :], kpos[:, :], slope),
                       waits=tuple(wts), incs=(("s_dve", 1),))
            wts2 = [("s_c2", c_p2setup)]
            if head_pe[hb]:
                wts2.append(("s_pe", head_pe[hb]))
            ca2 = p.op("vector",
                       lambda e, hb=hb, slope=slope: e.tensor_scalar(
                           shiftb[:, hb, :], posq_sb[:, :], -slope * SQD,
                           SHIFT_C * SQD, ALU.mult, ALU.subtract),
                       waits=tuple(wts2), incs=(("s_dve", 1),))

            wt_h = _WT[h]
            for grp in range(4):
                nkt = 8 * grp + 8
                kt_list = [kt for kt in range(nkt)
                           if min(3, (kt + wt_h - 8 * grp) // 2) >= (
                               0 if kt - 8 * grp < 1 else
                               math.ceil((kt - 8 * grp - 1) / 2))]
                exp_of = {}
                rng_of = {}
                for j in range(len(kt_list) + 1):
                    if j < len(kt_list):
                        kt = kt_list[j]
                        d = kt - 8 * grp
                        imin = 0 if d < 1 else math.ceil((d - 1) / 2)
                        ihi = min(3, (kt + wt_h - 8 * grp) // 2)
                        rng_of[kt] = (imin, ihi)
                        cols = slice(imin * 128, (ihi + 1) * 128)
                        qc0 = grp * 512 + imin * 128
                        qc1 = grp * 512 + (ihi + 1) * 128
                        sbk = j % 2
                        wts = [(sh_, c_hload), ("s_dve", ca2)]
                        if st_exp[sbk]:
                            wts.append(("s_act", st_exp[sbk]))
                        def pe_qk(e, hb=hb, kt=kt, cols=cols, qc0=qc0,
                                  qc1=qc1, sbk=sbk):
                            e.matmul(PS[sbk][:, cols],
                                     khT[:, hb, kt * 128:(kt + 1) * 128],
                                     qhT[:, hb, qc0:qc1],
                                     start=True, stop=False)
                            return e.matmul(PS[sbk][:, cols], ones_b[:, :],
                                            shiftb[:, hb, qc0:qc1],
                                            start=False, stop=True)
                        cqk = p.op("tensor", pe_qk, waits=tuple(wts),
                                   incs=(("s_pe", 1),))
                        cmask = None
                        if d >= 0:
                            cmask = p.op("vector",
                                         lambda e, sbk=sbk, imin=imin, d=d:
                                         e.tensor_add(
                                             PS[sbk][:, imin * 128:imin * 128 + 128],
                                             PS[sbk][:, imin * 128:imin * 128 + 128],
                                             maskd_sb[:, d % 2, :]),
                                         waits=(("s_pe", cqk),
                                                ("s_c2", c_p2setup)),
                                         incs=(("s_dve", 1),))
                        wts = [("s_pe", cqk), ("s_dve", ca1)]
                        if cmask:
                            wts.append(("s_dve", cmask))
                        if pt_pv[sbk]:
                            wts.append(("s_pe", pt_pv[sbk]))
                        cexp = p.op("scalar",
                                    lambda e, sbk=sbk, cols=cols, hb=hb, kt=kt:
                                    e.activation(ptb[:, sbk, cols],
                                                 PS[sbk][:, cols], AF.Exp,
                                                 bias=alib[:, hb, kt:kt + 1],
                                                 scale=SCALE),
                                    waits=tuple(wts), incs=(("s_act", 1),))
                        st_exp[sbk] = cexp
                        exp_of[kt] = cexp
                    if j >= 1:
                        kt = kt_list[j - 1]
                        sbk = (j - 1) % 2
                        imin, ihi = rng_of[kt]
                        wts = [("s_act", exp_of[kt])]
                        if j == 1 and pv_dve:
                            wts.append(("s_dve", pv_dve))
                        fko = {i: max(0, 8 * grp + 2 * i - wt_h)
                               for i in range(4)}
                        def pe_pv(e, sbk=sbk, imin=imin, ihi=ihi, kt=kt,
                                  grp=grp, hb=hb, fko=fko):
                            for i in range(imin, ihi + 1):
                                ins = e.matmul(PS[2 + i][:, 0:129],
                                               ptb[:, sbk, i * 128:(i + 1) * 128],
                                               vhE[:, hb, kt, 0:129],
                                               start=(kt == fko[i]),
                                               stop=(kt == 8 * grp + 2 * i + 1))
                            return ins
                        cpv = p.op("tensor", pe_pv, waits=tuple(wts),
                                   incs=(("s_pe", 1),))
                        pt_pv[sbk] = cpv
                ab = (h * 4 + grp) % 2
                sat = f"s_at{ab}"
                wts = [("s_pe", pt_pv[(len(kt_list) - 1) % 2])]
                if at_st[ab]:
                    wts.append((sat, at_st[ab]))
                def dve_recip(e):
                    for i in range(4):
                        ins = e.reciprocal(rd_sb[:, i:i + 1],
                                           PS[2 + i][:, 128:129])
                    return ins
                crec = p.op("vector", dve_recip, waits=tuple(wts),
                            incs=(("s_dve", 1),))
                def dve_tail(e, ab=ab):
                    for i in range(4):
                        ins = e.tensor_scalar(atb[:, ab, i, :],
                                              PS[2 + i][:, 0:128],
                                              rd_sb[:, i:i + 1], None, ALU.mult)
                    return ins
                ctail = p.op("vector", dve_tail, waits=(("s_dve", crec),),
                             incs=(("s_dve", 1),))
                pv_dve = ctail
                if _DBG_P2 and h == 0 and grp == 1:
                    def dbg_cp(e):
                        for i in range(4):
                            ins = e.activation(dbgT["pvsb"][:, i, 0:129],
                                               PS[2 + i][:, 0:129], AF.Identity)
                        return ins
                    cdc = p.op("scalar", dbg_cp, waits=(("s_dve", ctail),),
                               incs=(("s_act", 1),))
                    p.dma("sync", dbgT["pv"].ap(), dbgT["pvsb"][:, :, :],
                          "s_dbg", waits=(("s_act", cdc),))
                    p.dma("sync", dbgT["rd"].ap(), rd_sb[:, :], "s_dbg")
                    p.dma("sync", dbgT["pt"].ap()[0, :, :], ptb[:, 0, :],
                          "s_dbg")
                    p.dma("sync", dbgT["pt"].ap()[1, :, :], ptb[:, 1, :],
                          "s_dbg")
                    cfz = p.op("vector", lambda e: e.engine_nop(),
                               waits=(("s_dbg", p.ev("s_dbg")),),
                               incs=(("s_dve", 1),))
                    pv_dve = cfz
                aout = bass.AP(T["ats"], (grp * 512) * E + h * 128,
                               [[E, 128], [128 * E, 4], [1, 128]])
                cs = p.dma("sync", aout, atb[:, ab, :, :], sat,
                           waits=(("s_dve", ctail),))
                at_st[ab] = cs
            head_pe[hb] = max(pt_pv)
            al_war[hb] = max(st_exp)
        drain_prep(1000, trw2)
    c_p2 = [("s_at0", p.ev("s_at0")), ("s_at1", p.ev("s_at1"))]
    c_pw = ("s_pw", p.ev("s_pw"))
    c_pw1 = ("s_pw1", p.ev("s_pw1"))

    ctx3 = contextlib.ExitStack()
    with ctx3:
        e_ = ctx3.enter_context
        woS = e_(nc.sbuf_tensor("woS", [128, 16, 512], BF16))
        aTg = e_(nc.sbuf_tensor("aTg", [128, 16, 512], BF16))
        gatedH = e_(nc.sbuf_tensor("gatedH", [128, 32, 512], BF16))
        xn2T = e_(nc.sbuf_tensor("xn2T", [128, 16, 512], BF16))
        SCR = e_(nc.sbuf_tensor("SCR", [128, 4, E], F32))
        x1sb = e_(nc.sbuf_tensor("x1sb", [128, 2, E], F32))
        resid = e_(nc.sbuf_tensor("resid", [128, 2, E], F32))
        outsb = e_(nc.sbuf_tensor("outsb", [128, 2, E], F32))
        xn2b = e_(nc.sbuf_tensor("xn2b", [128, 2, E], BF16))
        gsil = e_(nc.sbuf_tensor("gsil", [128, 4, 512], BF16))
        wgs = e_(nc.sbuf_tensor("wgs", [128, 2, 2, 512], BF16))
        wus = e_(nc.sbuf_tensor("wus", [128, 2, 2, 512], BF16))
        wds = e_(nc.sbuf_tensor("wds", [128, 2, 4, 512], BF16))

        war = dict(woS=0, aTg=0, resid=[0, 0], outa0=0, outa1=0,
                   xn2b0=0, xn2b1=0, wo_evac=[0, 0], wgs=[0, 0], wus=[0, 0],
                   wds=[0, 0], x1st0=0, x1st1=0, outst=0, gated_pe=0,
                   dn_evac=0, sil=0, ups=0)

        for grp in range(4):
            c_xn2T = 0
            wts = list(c_p2)
            if war["aTg"]:
                wts.append(("s_pe", war["aTg"]))
            for fc in range(16):
                p.dma("sync", aTg[:, fc, :],
                      T["ats"].ap()[grp * 512:(grp + 1) * 512,
                                    fc * 128:(fc + 1) * 128],
                      "s_aT0", waits=tuple(wts) if fc == 0 else (),
                      transpose=True)
            c_aT0 = p.ev("s_aT0")
            wo_evac_tile = [0, 0, 0, 0]
            for d2g in range(4):
                wts = [("s_aT0", c_aT0), c_pw, c_pw1]
                if war["woS"]:
                    wts.append(("s_pe", war["woS"]))
                cwo = p.dma("sync", woS[:, :, :],
                            bass.AP(T["woTd"], d2g * 512,
                                    [[E, 128], [128 * E, 16], [1, 512]]),
                            "s_wo", waits=tuple(wts))
                for ti in range(4):
                    pb = (d2g * 4 + ti) % 2
                    wts = [("s_wo", cwo), ("s_const", c_setup)]
                    if war["wo_evac"][pb]:
                        wts.append(("s_act", war["wo_evac"][pb]))
                    def pe_wo(e, ti=ti, pb=pb, d2g=d2g):
                        for fc in range(16):
                            e.matmul(PS[pb][:, :],
                                     aTg[:, fc, ti * 128:(ti + 1) * 128],
                                     woS[:, fc, :],
                                     start=(fc == 0), stop=False)
                        return e.matmul(PS[pb][:, :], ones_b[:, :],
                                        borow[:, d2g * 512:(d2g + 1) * 512],
                                        start=False, stop=True)
                    cm = p.op("tensor", pe_wo, waits=tuple(wts),
                              incs=(("s_pe", 1),))
                    war["woS"] = cm
                    ce = p.op("scalar",
                              lambda e, pb=pb, ti=ti, d2g=d2g: e.activation(
                                  SCR[:, ti, d2g * 512:(d2g + 1) * 512],
                                  PS[pb][:, :], AF.Identity),
                              waits=(("s_pe", cm),), incs=(("s_act", 1),))
                    war["wo_evac"][pb] = ce
                    wo_evac_tile[ti] = ce
            war["aTg"] = war["woS"]
            for half2 in range(2):
                tis = [half2 * 2, half2 * 2 + 1]
                cxqs = {}
                jobs = []
                for ti in tis:
                    tt = grp * 4 + ti
                    rb = ti % 2
                    wts = []
                    if war["resid"][rb]:
                        wts.append(("s_dve", war["resid"][rb]))
                    cxqs[ti] = p.dma("sync", resid[:, rb, :],
                                     ap("xq")[tt * 128:(tt + 1) * 128, :],
                                     f"s_rs{rb}", waits=tuple(wts))
                    extra = []
                    if war[f"outa{rb}"]:
                        extra.append(("s_dve", war[f"outa{rb}"]))
                    jobs.append((SCR[:, ti, :], outsb[:, rb, :],
                                 20 + rb,
                                 rb, (("s_act", wo_evac_tile[ti]),),
                                 tuple(extra)))
                cnos = ln_pair(jobs)
                cx1s = {}
                for k, ti in enumerate(tis):
                    tt = grp * 4 + ti
                    rb = ti % 2
                    wts = [("s_act", cnos[k]), (f"s_rs{rb}", cxqs[ti])]
                    if war[f"x1st{rb}"]:
                        wts.append(("s_x1s" if rb == 0 else "s_gi",
                                    war[f"x1st{rb}"]))
                    if war["outst"]:
                        wts.append(("s_out", war["outst"]))
                    cx1 = p.op("vector",
                               lambda e, rb=rb: e.tensor_add(x1sb[:, rb, :],
                                                             outsb[:, rb, :],
                                                             resid[:, rb, :]),
                               waits=tuple(wts), incs=(("s_dve", 1),))
                    war[f"outa{rb}"] = cx1
                    war["resid"][rb] = cx1
                    cx1s[ti] = cx1
                    war[f"x1st{rb}"] = p.dma(
                        "sync", T["x1s"].ap()[tt * 128:(tt + 1) * 128, :],
                        x1sb[:, rb, :], "s_x1s" if rb == 0 else "s_gi",
                        waits=(("s_dve", cx1),))
                jobs = []
                for ti in tis:
                    rb = ti % 2
                    extra = []
                    if war[f"xn2b{rb}"]:
                        extra.append(("s_pe", war[f"xn2b{rb}"]))
                    jobs.append((x1sb[:, rb, :], xn2b[:, rb, :], 38 + rb,
                                 rb, (("s_dve", cx1s[ti]),), tuple(extra)))
                cn2s = ln_pair(jobs)
                for k, ti in enumerate(tis):
                    rb = ti % 2
                    cpe2, cdv2 = transpose_tile(xn2b[:, rb, :], xn2T, ti,
                                                cn2s[k])
                    war[f"xn2b{rb}"] = cpe2
                    c_xn2T = max(c_xn2T, cdv2)
            for half in range(2):
                for hbi in range(8):
                    hbk = half * 8 + hbi
                    gld = [0] * 8
                    uld = [0] * 8
                    gmm = [0] * 8
                    umm = [0] * 8

                    def wload(dstT, dst_sb, ecq, wb, warkey, sem_pfx,
                              extra_pe=0):
                        wts = [c_pw, c_pw1]
                        if extra_pe:
                            wts.append(("s_pe", extra_pe))
                        elif war[warkey][wb]:
                            wts.append(("s_pe", war[warkey][wb]))
                        return p.dma(
                            "gpsimd", dst_sb[:, wb, :, :],
                            bass.AP(dstT, ecq * 2 * 128 * FF + hbk * 512,
                                    [[FF, 128], [128 * FF, 2], [1, 512]]),
                            f"{sem_pfx}{wb}", waits=tuple(wts))

                    for c0 in range(2):
                        gld[c0] = wload(T["wgTd"], wgs, c0, c0 % 2, "wgs", "s_wg")
                        uld[c0] = wload(T["wuTd"], wus, c0, c0 % 2, "wus", "s_wu")
                    for ecq in range(8):
                        wts = [(f"s_wg{ecq % 2}", gld[ecq]), ("s_dve", c_xn2T)]
                        if ecq == 0:
                            if war["dn_evac"]:
                                wts.append(("s_dve", war["dn_evac"]))
                            if war["sil"]:
                                wts.append(("s_act", war["sil"]))
                            if war["ups"]:
                                wts.append(("s_dve", war["ups"]))
                        def pe_g(e, ecq=ecq):
                            ins = None
                            for ecl in range(2):
                                ec = ecq * 2 + ecl
                                for hcl in range(4):
                                    ins = e.matmul(
                                        PS[2 + hcl][:, :],
                                        wgs[:, ecq % 2, ecl,
                                            hcl * 128:(hcl + 1) * 128],
                                        xn2T[:, ec, :],
                                        start=(ec == 0), stop=(ec == 15))
                            return ins
                        gmm[ecq] = p.op("tensor", pe_g, waits=tuple(wts),
                                        incs=(("s_pe", 1),))
                        if ecq < 6:
                            gld[ecq + 2] = wload(T["wgTd"], wgs, ecq + 2,
                                                 ecq % 2, "wgs", "s_wg",
                                                 extra_pe=gmm[ecq])
                    war["wgs"][0] = war["wgs"][1] = gmm[7]
                    def act_sil(e):
                        for hcl in range(4):
                            ins = e.activation(gsil[:, hcl, :],
                                               PS[2 + hcl][:, :], AF.Silu)
                        return ins
                    csil = p.op("scalar", act_sil, waits=(("s_pe", gmm[7]),),
                                incs=(("s_act", 1),))
                    war["sil"] = csil
                    for ecq in range(8):
                        wts = [(f"s_wu{ecq % 2}", uld[ecq])]
                        if ecq == 0:
                            wts.append(("s_act", csil))
                        def pe_u(e, ecq=ecq):
                            ins = None
                            for ecl in range(2):
                                ec = ecq * 2 + ecl
                                for hcl in range(4):
                                    ins = e.matmul(
                                        PS[2 + hcl][:, :],
                                        wus[:, ecq % 2, ecl,
                                            hcl * 128:(hcl + 1) * 128],
                                        xn2T[:, ec, :],
                                        start=(ec == 0), stop=(ec == 15))
                            return ins
                        umm[ecq] = p.op("tensor", pe_u, waits=tuple(wts),
                                        incs=(("s_pe", 1),))
                        if ecq < 6:
                            uld[ecq + 2] = wload(T["wuTd"], wus, ecq + 2,
                                                 ecq % 2, "wus", "s_wu",
                                                 extra_pe=umm[ecq])
                    war["wus"][0] = war["wus"][1] = umm[7]
                    def dve_gated(e, hbi=hbi):
                        for hcl in range(4):
                            ins = e.tensor_mul(gatedH[:, hbi * 4 + hcl, :],
                                               gsil[:, hcl, :],
                                               PS[2 + hcl][:, :])
                        return ins
                    wts = [("s_pe", umm[7]), ("s_act", csil)]
                    if war["gated_pe"]:
                        wts.append(("s_pe", war["gated_pe"]))
                    cgd = p.op("vector", dve_gated, waits=tuple(wts),
                               incs=(("s_dve", 1),))
                    war["ups"] = cgd
                c_gated = war["ups"]
                for d2g in range(4):
                    for hcq in range(8):
                        wb = hcq % 2
                        wts = [c_pw, c_pw1]
                        if war["wds"][wb]:
                            wts.append(("s_pe", war["wds"][wb]))
                        cwd = p.dma("gpsimd", wds[:, wb, :, :],
                                    bass.AP(T["wdTd"],
                                            (half * 32 + hcq * 4) * 128 * E
                                            + d2g * 512,
                                            [[E, 128], [128 * E, 4], [1, 512]]),
                                    f"s_wd{wb}", waits=tuple(wts))
                        wts = [(f"s_wd{wb}", cwd), ("s_dve", c_gated)]
                        if hcq == 0 and war["dn_evac"]:
                            wts.append(("s_dve", war["dn_evac"]))
                        def pe_dn(e, wb=wb, hcq=hcq):
                            ins = None
                            for hcl in range(4):
                                hc = hcq * 4 + hcl
                                for ti in range(4):
                                    ins = e.matmul(
                                        PS[2 + ti][:, :],
                                        gatedH[:, hc, ti * 128:(ti + 1) * 128],
                                        wds[:, wb, hcl, :],
                                        start=(hc == 0), stop=(hc == 31))
                            return ins
                        cdm = p.op("tensor", pe_dn, waits=tuple(wts),
                                   incs=(("s_pe", 1),))
                        war["wds"][wb] = cdm
                        if d2g == 3 and hcq == 7:
                            war["gated_pe"] = cdm
                    def dve_dn(e, d2g=d2g, half=half):
                        for ti in range(4):
                            if half == 0:
                                ins = e.tensor_copy(
                                    SCR[:, ti, d2g * 512:(d2g + 1) * 512],
                                    PS[2 + ti][:, :])
                            else:
                                ins = e.tensor_add(
                                    SCR[:, ti, d2g * 512:(d2g + 1) * 512],
                                    SCR[:, ti, d2g * 512:(d2g + 1) * 512],
                                    PS[2 + ti][:, :])
                        return ins
                    cde = p.op("vector", dve_dn,
                               waits=(("s_pe", max(war["wds"])),),
                               incs=(("s_dve", 1),))
                    war["dn_evac"] = cde
            c_ffn = war["dn_evac"]
            for half2 in range(2):
                tis = [half2 * 2, half2 * 2 + 1]
                cx1rs = {}
                jobs = []
                for ti in tis:
                    tt = grp * 4 + ti
                    rb = ti % 2
                    wts = [("s_x1s" if rb == 0 else "s_gi",
                            war[f"x1st{rb}"])]
                    if war["resid"][rb]:
                        wts.append(("s_dve", war["resid"][rb]))
                    cx1rs[ti] = p.dma("sync", resid[:, rb, :],
                                      T["x1s"].ap()[tt * 128:(tt + 1) * 128, :],
                                      f"s_rs{rb}", waits=tuple(wts))
                    extra = []
                    if war[f"outa{rb}"]:
                        extra.append(("s_dve", war[f"outa{rb}"]))
                    jobs.append((SCR[:, ti, :], outsb[:, rb, :], 20 + rb,
                                 rb, (("s_dve", c_ffn),), tuple(extra)))
                cnfs = ln_pair(jobs)
                for k, ti in enumerate(tis):
                    tt = grp * 4 + ti
                    rb = ti % 2
                    wts = [("s_act", cnfs[k]), (f"s_rs{rb}", cx1rs[ti]),
                           ("s_x1s" if rb == 0 else "s_gi",
                            war[f"x1st{rb}"])]
                    if war["outst"]:
                        wts.append(("s_out", war["outst"]))
                    cfin = p.op("vector",
                                lambda e, rb=rb: e.tensor_add(x1sb[:, rb, :],
                                                              outsb[:, rb, :],
                                                              resid[:, rb, :]),
                                waits=tuple(wts), incs=(("s_dve", 1),))
                    war[f"outa{rb}"] = cfin
                    war["resid"][rb] = cfin
                    war[f"x1st{rb}"] = war[f"x1st{rb}"]
                    cso = p.dma("sync", ap("out")[tt * 128:(tt + 1) * 128, :],
                                x1sb[:, rb, :], "s_out",
                                waits=(("s_dve", cfin),))
                    war["outst"] = cso
                    war[f"xf{rb}"] = cfin

        final = p.ev("s_out")
        dprev = 0
        for name in _DBG_TAPS:
            t = T[name]
            o = nc.dram_tensor("dbg_" + name, list(t.shape), t.dtype,
                               kind="ExternalOutput")
            w = [("s_out", final)]
            if dprev:
                w.append(("s_dbg", dprev))
            dprev = p.dma("sync", o.ap(), t.ap(), "s_dbg", waits=tuple(w))
        p.op("sync", lambda e: None,
             waits=(("s_out", final), ("s_dbg", p.ev("s_dbg"))))


_NC_CACHE = {}


def _get_nc():
    key = ("nc", _DBG_P1, _DBG_P2) + tuple(_DBG_TAPS)
    if key not in _NC_CACHE:
        _NC_CACHE[key] = build_nc()
    return _NC_CACHE[key]


def kernel(L0_attn_x, L0_attn_Wq, L0_attn_bq, L0_attn_Wk, L0_attn_bk,
           L0_attn_Wv, L0_attn_bv, L0_attn_Wo, L0_attn_bo,
           L0_ffn_w_gate, L0_ffn_w_up, L0_ffn_w_down):
    bf = ml_dtypes.bfloat16
    x = np.ascontiguousarray(np.asarray(L0_attn_x, dtype=np.float32))
    eye = np.eye(128, dtype=bf)
    tri = np.where(np.arange(128)[:, None] > np.arange(128)[None, :],
                   np.float32(-1e30), np.float32(0.0)).astype(bf)
    fullm = np.full((128, 128), -1e30, dtype=bf)
    zerom = np.zeros((128, 128), dtype=bf)
    maskd_by_p = [np.ascontiguousarray(np.stack([tri, fullm])),
                  np.ascontiguousarray(np.stack([zerom, tri]))]

    nc = _get_nc()
    w32 = lambda a: np.ascontiguousarray(np.asarray(a, np.float32))
    weights = {"wq": w32(L0_attn_Wq), "wk": w32(L0_attn_Wk),
               "wv": w32(L0_attn_Wv), "wo": w32(L0_attn_Wo),
               "bq": w32(L0_attn_bq), "bk": w32(L0_attn_bk),
               "bv": w32(L0_attn_bv), "bo": w32(L0_attn_bo),
               "wg": w32(L0_ffn_w_gate), "wu": w32(L0_ffn_w_up),
               "wd": w32(L0_ffn_w_down)}
    in_maps = []
    for c in range(8):
        b, par = c // 2, c % 2
        xq_c = np.ascontiguousarray(
            x[b].reshape(NT, 128, E)[par::2].reshape(TOWN, E))
        posq_c = (np.arange(TOWN) % 128 + 256.0 * (np.arange(TOWN) // 128)
                  + 128.0 * par).astype(np.float32).reshape(1, TOWN)
        m = dict(weights)
        posk_c = (np.arange(128)[:, None] + 128.0 * np.arange(NT)[None, :]
                  ).astype(np.float32)
        m.update({"xb": np.ascontiguousarray(x[b]), "xq": xq_c,
                  "posq": posq_c, "posk": np.ascontiguousarray(posk_c),
                  "maskd": maskd_by_p[par], "eye": eye})
        in_maps.append(m)
    res = run_bass_kernel_spmd(nc, in_maps, core_ids=list(range(8)))
    del _LAST_RES[:]
    _LAST_RES.append(res)
    full_out = np.empty((B, S, E), dtype=np.float32)
    for c in range(8):
        b, par = c // 2, c % 2
        full_out[b].reshape(NT, 128, E)[par::2] = \
            np.asarray(res.results[c]["out"]).reshape(NQ, 128, E)
    return full_out


# version 16 (final)
# speedup vs baseline: 1.0403x; 1.0403x over previous
import math
import contextlib
import numpy as np
import ml_dtypes
import concourse.bass as bass
import concourse.mybir as mybir
from concourse.bass_utils import run_bass_kernel_spmd

F32 = mybir.dt.float32
BF16 = mybir.dt.bfloat16
AF = mybir.ActivationFunctionType
ALU = mybir.AluOpType
AX = mybir.AxisListType

B, S, E, H, D, FF = 4, 4096, 2048, 16, 128, 8192
NT = S // 128
NQ = NT // 2
TOWN = NQ * 128
SCALE = 1.0 / math.sqrt(D)
SQD = math.sqrt(D)
SHIFT_C = 10.0
EPS = 1e-5
_SLOPES = [2.0 ** (-0.5 * (i + 1)) for i in range(H)]
ALIBI_WIN = 45.0
_WT = [min(NT, math.ceil((ALIBI_WIN / sl + 127.0) / 128) + 1) for sl in _SLOPES]
_DBG_TAPS = ()
_DBG_P1 = False
_DBG_P2 = False
_LAST_RES = []


class Prog:
    def __init__(self):
        self.items = []
        self.cnt = {}

    def ev(self, sem):
        return self.cnt.get(sem, 0)

    def op(self, eng, fn, waits=(), incs=()):
        waits = [(s, c) for (s, c) in waits if c and c > 0]
        for s, c in waits:
            assert self.cnt.get(s, 0) >= c, (eng, s, c, self.cnt.get(s, 0))
        for s, a in incs:
            self.cnt[s] = self.cnt.get(s, 0) + a
        self.items.append((eng, tuple(waits), fn, tuple(incs)))
        return self.cnt[incs[0][0]] if incs else None

    def dma(self, eng, out, in_, sem, waits=(), transpose=False, slow=False):
        def fn(e, out=out, in_=in_):
            if transpose:
                return e.dma_start_transpose(out, in_)
            if slow:
                return e.dma_start(out, in_, allow_slow_non_contiguous=True)
            return e.dma_start(out, in_)
        return self.op(eng, fn, waits=waits, incs=((sem, 16),))


SEMS = ["s_g", "s_gi", "s_pr1", "s_pw1", "s_const", "s_x0", "s_x1", "s_wres", "s_pr", "s_pw",
        "s_ksp0", "s_ksp1", "s_vsp0", "s_vsp1", "s_c2", "s_h0", "s_h1",
        "s_at0", "s_at1", "s_aT0", "s_aT1", "s_rs0", "s_rs1", "s_wo",
        "s_wg0", "s_wg1", "s_wu0", "s_wu1", "s_wd0", "s_wd1", "s_x1s",
        "s_out", "s_dbg", "s_pe", "s_act", "s_dve"]


def build_nc():
    nc = bass.Bass("TRN2", target_bir_lowering=False)
    p = Prog()

    T = {}
    def dram(name, shape, dt, kind=None):
        t = nc.dram_tensor(name, shape, dt, kind=kind) if kind else \
            nc.dram_tensor(name, shape, dt)
        T[name] = t
        return t

    for n, sh in (("xb", [S, E]), ("xq", [TOWN, E]), ("wq", [E, E]),
                  ("wk", [E, E]), ("wv", [E, E]), ("wo", [E, E]),
                  ("wg", [FF, E]), ("wu", [FF, E]), ("wd", [E, FF]),
                  ("posq", [1, TOWN]), ("posk", [128, NT])):
        dram(n, sh, F32, "ExternalInput")
    dram("maskd", [2, 128, 128], BF16, "ExternalInput")
    for n in ("bq", "bk", "bv", "bo"):
        dram(n, [E], F32, "ExternalInput")
    dram("eye", [128, 128], BF16, "ExternalInput")
    dram("out", [TOWN, E], F32, "ExternalOutput")

    for n, sh in (("wkb", [E, E]), ("wvb", [E, E]), ("wqb", [E, E]),
                  ("wob", [E, E]), ("wgb", [FF, E]), ("wub", [FF, E]),
                  ("wdb", [E, FF]),
                  ("woTd", [16, 128, E]), ("wgTd", [16, 128, FF]),
                  ("wuTd", [16, 128, FF]), ("wdTd", [64, 128, E]),
                  ("kts", [E, S]), ("qts", [E, TOWN]),
                  ("vts", [NT, H, 128, 128]), ("ats", [TOWN, E])):
        dram(n, sh, BF16)
    dram("x1s", [TOWN, E], F32)

    ctx = contextlib.ExitStack()
    with ctx:
        e_ = ctx.enter_context
        PS = [e_(nc.psum_tensor(f"ps{i}", [128, 512], F32)) for i in range(6)]
        PT = [e_(nc.psum_tensor("pt6", [128, 1024], BF16)),
              e_(nc.psum_tensor("pt7", [128, 1024], BF16))]
        sb = {}
        for n, sh, dt in (("eye_sb", [128, 128], BF16),
                          ("ones_b", [1, 128], BF16),
                          ("ones_f", [1, 128], F32),
                          ("bk_sb", [128, 16], F32),
                          ("bq_sb", [128, 16], F32),
                          ("bvrow", [1, E], BF16),
                          ("borow", [1, E], BF16),
                          ("stats", [128, 64], F32),
                          ("sqs", [128, 2, E], BF16)):
            sb[n] = e_(nc.sbuf_tensor(n, sh, dt))
        sems = {n: e_(nc.semaphore(n)) for n in SEMS}

        _emit(nc, p, T, PS, PT, sb)

        with nc.Block() as block:
            for name, deco in (("sync", block.sync), ("gpsimd", block.gpsimd),
                               ("tensor", block.tensor), ("scalar", block.scalar),
                               ("vector", block.vector)):
                def runner(e, name=name):
                    for (eng, waits, fn, incs) in p.items:
                        if eng != name:
                            continue
                        for (s, c) in waits:
                            e.wait_ge(sems[s], c)
                        ins = fn(e)
                        for (s, a) in incs:
                            ins.then_inc(sems[s], a)
                deco(runner)
    return nc


def _emit(nc, p, T, PS, PT, sb):
    eye_sb, ones_b, ones_f = sb["eye_sb"], sb["ones_b"], sb["ones_f"]
    bk_sb, bq_sb = sb["bk_sb"], sb["bq_sb"]
    bvrow, borow = sb["bvrow"], sb["borow"]
    stats, sqs = sb["stats"], sb["sqs"]
    ap = lambda n: T[n].ap()

    p.dma("sync", eye_sb[:, :], ap("eye"), "s_const")
    p.dma("sync", bk_sb[:, :], bass.AP(T["bk"], 0, [[1, 128], [128, 16]]),
          "s_const", slow=True)
    p.dma("sync", bq_sb[:, :], bass.AP(T["bq"], 0, [[1, 128], [128, 16]]),
          "s_const", slow=True)
    c_setup = p.ev("s_const")
    p.dma("gpsimd", sb["bvrow"][:, :], ap("bv"), "s_gi")
    p.dma("gpsimd", sb["borow"][:, :], ap("bo"), "s_gi")
    c_rows = p.ev("s_gi")
    p.op("vector", lambda e: e.memset(ones_b[:, :], 1.0), incs=(("s_dve", 1),))
    p.op("vector", lambda e: e.memset(ones_f[:, :], 1.0), incs=(("s_dve", 1),))
    c_ones = p.ev("s_dve")

    cast_done = {}
    prev_cast = [0]
    def cast(name, chunks):
        src, dst = ap(name), T[name + "b"]
        rows = src.shape[0]
        per = rows // chunks
        for i in range(chunks):
            w = (("s_g", prev_cast[0]),) if (i == 0 and prev_cast[0]) else ()
            p.dma("gpsimd", dst.ap()[i * per:(i + 1) * per, :],
                  src[i * per:(i + 1) * per, :], "s_g", waits=w)
        cast_done[name] = prev_cast[0] = p.ev("s_g")
    for n, ch in (("wk", 2), ("wv", 2), ("wq", 2), ("wo", 2),
                  ("wg", 4), ("wu", 4), ("wd", 4)):
        cast(n, ch)

    prep_items = ([("wo", i, 0, 2048) for i in range(16)]
                  + [("wg", i, q, 2048) for i in range(16) for q in range(4)]
                  + [("wu", i, q, 2048) for i in range(16) for q in range(4)]
                  + [("wd", i, 0, 2048) for i in range(64)])
    prep_state = {"i": 0, "w": [0, 0]}

    def drain_prep(n, trw):
        meta = {"wo": (T["wob"], T["woTd"]), "wg": (T["wgb"], T["wgTd"]),
                "wu": (T["wub"], T["wuTd"]), "wd": (T["wdb"], T["wdTd"])}
        while n > 0 and prep_state["i"] < len(prep_items):
            w, idx, q, rows = prep_items[prep_state["i"]]
            prep_state["i"] += 1
            n -= 1
            srcb, dstT = meta[w]
            buf = prep_state["i"] % 2
            spr = "s_pr" if buf == 0 else "s_pr1"
            spw = "s_pw" if buf == 0 else "s_pw1"
            wts = [("s_g", cast_done[w])]
            if prep_state["w"][buf]:
                wts.append((spw, prep_state["w"][buf]))
            c1 = p.dma("sync", trw[:, buf, 0:rows],
                       srcb.ap()[q * rows:(q + 1) * rows,
                                 idx * 128:(idx + 1) * 128],
                       spr, waits=tuple(wts), transpose=True)
            prep_state["w"][buf] = p.dma(
                "sync", dstT.ap()[idx, :, q * rows:(q + 1) * rows],
                trw[:, buf, 0:rows], spw, waits=((spr, c1),))

    ln_state = {"sqs0": 0, "sqs1": 0}

    def ln(src_ap, dst_ap, sl, wait_src, extra_norm_waits=(), sq=0):
        k = f"sqs{sq}"
        wsq = wait_src + ((("s_dve", ln_state[k]),) if ln_state[k] else ())
        ca = p.op("scalar",
                  lambda e: e.activation(sqs[:, sq, :], src_ap, AF.Square),
                  waits=wsq, incs=(("s_act", 1),))
        def reduces(e):
            e.reduce_sum(stats[:, 0 + sl:1 + sl], src_ap, axis=AX.X)
            return e.reduce_sum(stats[:, 2 + sl:3 + sl], sqs[:, sq, :],
                                axis=AX.X)
        cd = p.op("vector", reduces, waits=wait_src + (("s_act", ca),),
                  incs=(("s_dve", 1),))
        ln_state[k] = cd
        cd1 = p.op("vector",
                   lambda e: (e.tensor_scalar(stats[:, 4 + sl:5 + sl],
                                              stats[:, 0 + sl:1 + sl],
                                              1.0 / E, None, ALU.mult),
                              e.tensor_scalar(stats[:, 6 + sl:7 + sl],
                                              stats[:, 2 + sl:3 + sl],
                                              1.0 / E, None, ALU.mult))[-1],
                   waits=(("s_dve", cd),), incs=(("s_dve", 1),))
        cd2 = p.op("vector",
                   lambda e: e.scalar_tensor_tensor(
                       stats[:, 8 + sl:9 + sl], stats[:, 4 + sl:5 + sl], -1.0,
                       stats[:, 4 + sl:5 + sl], ALU.mult, ALU.mult),
                   waits=(("s_dve", cd1),), incs=(("s_dve", 1),))
        cd3 = p.op("vector",
                   lambda e: e.scalar_tensor_tensor(
                       stats[:, 10 + sl:11 + sl], stats[:, 6 + sl:7 + sl], EPS,
                       stats[:, 8 + sl:9 + sl], ALU.add, ALU.add),
                   waits=(("s_dve", cd2),), incs=(("s_dve", 1),))
        ca1 = p.op("scalar",
                   lambda e: e.activation(stats[:, 12 + sl:13 + sl],
                                          stats[:, 10 + sl:11 + sl], AF.Ln),
                   waits=(("s_dve", cd3),), incs=(("s_act", 1),))
        cr = p.op("scalar",
                  lambda e: e.activation(stats[:, 14 + sl:15 + sl],
                                         stats[:, 12 + sl:13 + sl], AF.Exp,
                                         scale=-0.5),
                  waits=(("s_act", ca1),), incs=(("s_act", 1),))
        cn = p.op("vector",
                  lambda e: e.scalar_tensor_tensor(
                      stats[:, 16 + sl:17 + sl], stats[:, 4 + sl:5 + sl], -1.0,
                      stats[:, 14 + sl:15 + sl], ALU.mult, ALU.mult),
                  waits=(("s_act", cr),), incs=(("s_dve", 1),))
        co = p.op("scalar",
                  lambda e: e.activation(dst_ap, src_ap, AF.Identity,
                                         bias=stats[:, 16 + sl:17 + sl],
                                         scale=stats[:, 14 + sl:15 + sl]),
                  waits=(("s_dve", cn), ("s_act", cr)) + tuple(extra_norm_waits),
                  incs=(("s_act", 1),))
        return co

    tr_war = [0, 0]

    def transpose_tile(xn_ap, xnT, ti, wait_act):
        last = 0
        pe_last = 0
        for half in range(2):
            def pe_fn(e, half=half):
                for j in range(8):
                    ins = e.transpose(PT[half][:, j * 128:(j + 1) * 128],
                                      xn_ap[:, (half * 8 + j) * 128:
                                            (half * 8 + j + 1) * 128],
                                      eye_sb[:, :])
                return ins
            wts = [("s_act", wait_act), ("s_const", c_setup)]
            if tr_war[half]:
                wts.append(("s_dve", tr_war[half]))
            ct = p.op("tensor", pe_fn, waits=tuple(wts), incs=(("s_pe", 1),))
            cc = p.op("vector",
                      lambda e, half=half, ti=ti: e.tensor_copy(
                          xnT[:, half * 8:half * 8 + 8, ti * 128:(ti + 1) * 128],
                          PT[half][:, 0:1024]),
                      waits=(("s_pe", ct),), incs=(("s_dve", 1),))
            tr_war[half] = cc
            last = cc
            pe_last = ct
        return pe_last, last

    ctx1 = contextlib.ExitStack()
    with ctx1:
        e_ = ctx1.enter_context
        xt = e_(nc.sbuf_tensor("xt", [128, 2, E], F32))
        xn1 = e_(nc.sbuf_tensor("xn1", [128, 4, E], BF16))
        xn1T = e_(nc.sbuf_tensor("xn1T", [128, 16, 512], BF16))
        ksb = e_(nc.sbuf_tensor("ksb", [128, 2, 512], BF16))
        vsb = e_(nc.sbuf_tensor("vsb", [128, 1, E], BF16))
        trw = e_(nc.sbuf_tensor("trw", [128, 2, 2048], BF16))

        st = dict(norm=[0, 0], kevac=[0, 0], kst=[0, 0], vevac=[0, 0],
                  vst=[0, 0], trpe=[0, 0, 0, 0])
        if _DBG_P1:
            dbg_xt = nc.dram_tensor("dbg_xt", [8, 128, E], F32,
                                    kind="ExternalOutput")
            dbg_xn1 = nc.dram_tensor("dbg_xn1", [8, 128, E], BF16,
                                     kind="ExternalOutput")
            dbg_sts = nc.dram_tensor("dbg_sts", [8, 128, 40], F32,
                                     kind="ExternalOutput")

        def p1_loads_ln(src_ap, grp):
            norms = []
            for ti in range(4):
                tt = grp * 4 + ti
                sl = tt % 2
                sx = f"s_x{sl}"
                cx = p.dma("gpsimd", xt[:, sl, :],
                           src_ap[tt * 128:(tt + 1) * 128, :], sx,
                           waits=(("s_act", st["norm"][sl]),))
                extra = [("s_pe", st["trpe"][ti])] if st["trpe"][ti] else []
                cn = ln(xt[:, sl, :], xn1[:, ti, :], sl, ((sx, cx),),
                        extra_norm_waits=extra, sq=sl)
                st["norm"][sl] = cn
                norms.append(cn)
            return norms

        def p1_transposes(norms, xnT):
            last = 0
            for ti in range(4):
                cpe, cdv = transpose_tile(xn1[:, ti, :], xnT, ti, norms[ti])
                st["trpe"][ti] = cpe
                last = max(last, cdv)
            return last

        def proj_feature(wT_sb, bias_col, dst_dram, grp, c_ready, c_w):
            for fc in range(16):
                bsl = fc % 2
                ssp = f"s_ksp{bsl}"
                wts = [("s_dve", c_ready), ("s_wres", c_w)]
                if st["kevac"][bsl]:
                    wts.append(("s_act", st["kevac"][bsl]))
                def pe_fn(e, fc=fc, bsl=bsl, wT_sb=wT_sb):
                    for ec in range(16):
                        ins = e.matmul(PS[bsl][:, :],
                                       wT_sb[:, ec, fc * 128:(fc + 1) * 128],
                                       xn1T[:, ec, :],
                                       start=(ec == 0), stop=(ec == 15))
                    return ins
                cm = p.op("tensor", pe_fn, waits=tuple(wts), incs=(("s_pe", 1),))
                wts2 = [("s_pe", cm)]
                if st["kst"][bsl]:
                    wts2.append((ssp, st["kst"][bsl]))
                ce = p.op("scalar",
                          lambda e, fc=fc, bsl=bsl, bc=bias_col: e.activation(
                              ksb[:, bsl, :], PS[bsl][:, :], AF.Identity,
                              bias=bc[:, fc:fc + 1]),
                          waits=tuple(wts2), incs=(("s_act", 1),))
                st["kevac"][bsl] = ce
                cs = p.dma("sync",
                           dst_dram.ap()[fc * 128:(fc + 1) * 128,
                                         grp * 512:(grp + 1) * 512],
                           ksb[:, bsl, :], ssp, waits=(("s_act", ce),))
                st["kst"][bsl] = cs

        def proj_v(wvT, grp, c_ready, c_w):
            for ti in range(4):
                tt = grp * 4 + ti
                vsl = 0
                ssp = "s_vsp0"
                wts = [("s_dve", c_ready), ("s_wres", c_w),
                       ("s_const", c_setup), ("s_gi", c_rows)]
                vmax = max(st["vevac"])
                if vmax:
                    wts.append(("s_act", vmax))
                def pe_fn(e, ti=ti, wvT=wvT):
                    for fg in range(4):
                        for ec in range(16):
                            e.matmul(PS[2 + fg][:, :],
                                     xn1T[:, ec, ti * 128:(ti + 1) * 128],
                                     wvT[:, ec, fg * 512:(fg + 1) * 512],
                                     start=(ec == 0), stop=False)
                        ins = e.matmul(PS[2 + fg][:, :], ones_b[:, :],
                                       bvrow[:, fg * 512:(fg + 1) * 512],
                                       start=False, stop=True)
                    return ins
                cm = p.op("tensor", pe_fn, waits=tuple(wts), incs=(("s_pe", 1),))
                wts2 = [("s_pe", cm)]
                if st["vst"][vsl]:
                    wts2.append((ssp, st["vst"][vsl]))
                def act_fn(e, vsl=vsl):
                    for fg in range(4):
                        ins = e.activation(vsb[:, vsl, fg * 512:(fg + 1) * 512],
                                           PS[2 + fg][:, :], AF.Identity)
                    return ins
                ce = p.op("scalar", act_fn, waits=tuple(wts2), incs=(("s_act", 1),))
                st["vevac"][vsl] = ce
                vout = bass.AP(T["vts"], tt * H * 128 * 128,
                               [[128, 128], [128 * 128, H], [1, 128]])
                cs = p.dma("sync", vout, vsb[:, vsl, :], ssp,
                           waits=(("s_act", ce),))
                st["vst"][vsl] = cs

        with nc.sbuf_tensor("wkT", [128, 16, E], BF16) as wkT, \
             nc.sbuf_tensor("wvT", [128, 16, E], BF16) as wvT:
            for ec in range(16):
                p.dma("sync", wkT[:, ec, :],
                      T["wkb"].ap()[0:E, ec * 128:(ec + 1) * 128], "s_wres",
                      waits=(("s_g", cast_done["wk"]),), transpose=True)
            for ec in range(16):
                p.dma("sync", wvT[:, ec, :],
                      T["wvb"].ap()[0:E, ec * 128:(ec + 1) * 128], "s_wres",
                      waits=(("s_g", cast_done["wv"]),), transpose=True)
            c_wkv = p.ev("s_wres")
            pend = p1_loads_ln(ap("xb"), 0)
            for grp in range(8):
                c_ready = p1_transposes(pend, xn1T)
                if grp < 7:
                    pend = p1_loads_ln(ap("xb"), grp + 1)
                proj_feature(wkT, bk_sb, T["kts"], grp, c_ready, c_wkv)
                proj_v(wvT, grp, c_ready, c_wkv)
                drain_prep(7, trw)

        with nc.sbuf_tensor("wqT", [128, 16, E], BF16) as wqT:
            wts = (("s_g", cast_done["wq"]), ("s_pe", p.ev("s_pe")),
                   ("s_act", p.ev("s_act")), ("s_dve", p.ev("s_dve")))
            for ec in range(16):
                p.dma("sync", wqT[:, ec, :],
                      T["wqb"].ap()[0:E, ec * 128:(ec + 1) * 128], "s_wres",
                      waits=wts if ec == 0 else (), transpose=True)
            c_wq = p.ev("s_wres")
            pend = p1_loads_ln(ap("xq"), 0)
            for grp in range(4):
                c_ready = p1_transposes(pend, xn1T)
                if grp < 3:
                    pend = p1_loads_ln(ap("xq"), grp + 1)
                proj_feature(wqT, bq_sb, T["qts"], grp, c_ready, c_wq)
                drain_prep(8, trw)
    c_p1 = [("s_ksp0", p.ev("s_ksp0")), ("s_ksp1", p.ev("s_ksp1")),
            ("s_vsp0", p.ev("s_vsp0")), ("s_vsp1", p.ev("s_vsp1")),
            ("s_pr", p.ev("s_pr")), ("s_pr1", p.ev("s_pr1")),
            ("s_pw", p.ev("s_pw")), ("s_pw1", p.ev("s_pw1"))]


    ctx2 = contextlib.ExitStack()
    with ctx2:
        e_ = ctx2.enter_context
        khT = e_(nc.sbuf_tensor("khT", [128, 2, S], BF16))
        qhT = e_(nc.sbuf_tensor("qhT", [128, 2, TOWN], BF16))
        vhE = e_(nc.sbuf_tensor("vhE", [128, 2, NT, 132], BF16))
        ptb = e_(nc.sbuf_tensor("ptb", [128, 2, 512], BF16))
        atb = e_(nc.sbuf_tensor("atb", [128, 2, 4, 128], BF16))
        posq_sb = e_(nc.sbuf_tensor("posq_sb", [1, TOWN], F32))
        maskd_sb = e_(nc.sbuf_tensor("maskd_sb", [128, 2, 128], BF16))
        kpos = e_(nc.sbuf_tensor("kpos", [128, NT], F32))
        alib = e_(nc.sbuf_tensor("alib", [128, 2, NT], F32))
        shiftb = e_(nc.sbuf_tensor("shiftb", [1, 2, TOWN], BF16))
        rd_sb = e_(nc.sbuf_tensor("rd_sb", [128, 8], F32))
        trw2 = e_(nc.sbuf_tensor("trw2", [128, 2, 2048], BF16))

        p.dma("sync", kpos[:, :], ap("posk"), "s_c2", waits=tuple(c_p1))
        p.dma("sync", posq_sb[:, :], ap("posq"), "s_c2")
        p.dma("sync", maskd_sb[:, :, :],
              bass.AP(T["maskd"], 0, [[128, 128], [128 * 128, 2], [1, 128]]),
              "s_c2")
        c_p2setup = p.ev("s_c2")

        cms = 0
        for bufi in range(2):
            cms = p.op("vector",
                       lambda e, bufi=bufi: e.memset(vhE[:, bufi, :, 128:129],
                                                     1.0),
                       waits=tuple(c_p1), incs=(("s_dve", 1),))

        dbgT = {}
        if _DBG_P2:
            dbgT["st"] = nc.dram_tensor("dbg_st", [2, 128, 512], F32,
                                        kind="ExternalOutput")
            dbgT["sh"] = nc.dram_tensor("dbg_sh", [1, 2 * TOWN], BF16,
                                        kind="ExternalOutput")
            dbgT["pq"] = nc.dram_tensor("dbg_pq", [1, TOWN], F32,
                                        kind="ExternalOutput")
            dbgT["on"] = nc.dram_tensor("dbg_on", [1, 128], BF16,
                                        kind="ExternalOutput")
            dbgT["stsb"] = e_(nc.sbuf_tensor("dbg_stsb", [128, 2, 512], F32))
            dbgT["pv"] = nc.dram_tensor("dbg_pv", [4, 128, 132], F32,
                                        kind="ExternalOutput")
            dbgT["rd"] = nc.dram_tensor("dbg_rd", [128, 8], F32,
                                        kind="ExternalOutput")
            dbgT["pt"] = nc.dram_tensor("dbg_pt", [2, 128, 512], BF16,
                                        kind="ExternalOutput")
            dbgT["pvsb"] = e_(nc.sbuf_tensor("dbg_pvsb", [128, 4, 132], F32))
        head_pe = [0, 0]
        st_exp = [0, 0]
        pt_pv = [0, 0]
        pv_dve = 0
        at_st = [0, 0]
        al_war = [0, 0]

        for h in range(H):
            drain_prep(10, trw2)
            hb = h % 2
            sh_ = f"s_h{hb}"
            slope = _SLOPES[h]
            wld = list(c_p1) + [("s_dve", cms)]
            if head_pe[hb]:
                wld.append(("s_pe", head_pe[hb]))
            p.dma("sync", khT[:, hb, :], T["kts"].ap()[h * 128:(h + 1) * 128, :],
                  sh_, waits=tuple(wld))
            p.dma("sync", qhT[:, hb, :], T["qts"].ap()[h * 128:(h + 1) * 128, :],
                  sh_)
            vin = bass.AP(T["vts"], h * 128 * 128,
                          [[128, 128], [H * 128 * 128, NT], [1, 128]])
            p.dma("sync", vhE[:, hb, :, 0:128], vin, sh_)
            c_hload = p.ev(sh_)
            wts = [("s_c2", c_p2setup)]
            if al_war[hb]:
                wts.append(("s_act", al_war[hb]))
            if head_pe[hb]:
                wts.append(("s_pe", head_pe[hb]))
            ca1 = p.op("vector",
                       lambda e, hb=hb, slope=slope: e.tensor_scalar_mul(
                           alib[:, hb, :], kpos[:, :], slope),
                       waits=tuple(wts), incs=(("s_dve", 1),))
            wts2 = [("s_c2", c_p2setup)]
            if head_pe[hb]:
                wts2.append(("s_pe", head_pe[hb]))
            ca2 = p.op("vector",
                       lambda e, hb=hb, slope=slope: e.tensor_scalar(
                           shiftb[:, hb, :], posq_sb[:, :], -slope * SQD,
                           SHIFT_C * SQD, ALU.mult, ALU.subtract),
                       waits=tuple(wts2), incs=(("s_dve", 1),))

            wt_h = _WT[h]
            for grp in range(4):
                nkt = 8 * grp + 8
                kt_list = [kt for kt in range(nkt)
                           if min(3, (kt + wt_h - 8 * grp) // 2) >= (
                               0 if kt - 8 * grp < 1 else
                               math.ceil((kt - 8 * grp - 1) / 2))]
                exp_of = {}
                rng_of = {}
                for j in range(len(kt_list) + 1):
                    if j < len(kt_list):
                        kt = kt_list[j]
                        d = kt - 8 * grp
                        imin = 0 if d < 1 else math.ceil((d - 1) / 2)
                        ihi = min(3, (kt + wt_h - 8 * grp) // 2)
                        rng_of[kt] = (imin, ihi)
                        cols = slice(imin * 128, (ihi + 1) * 128)
                        qc0 = grp * 512 + imin * 128
                        qc1 = grp * 512 + (ihi + 1) * 128
                        sbk = j % 2
                        wts = [(sh_, c_hload), ("s_dve", ca2)]
                        if st_exp[sbk]:
                            wts.append(("s_act", st_exp[sbk]))
                        def pe_qk(e, hb=hb, kt=kt, cols=cols, qc0=qc0,
                                  qc1=qc1, sbk=sbk):
                            e.matmul(PS[sbk][:, cols],
                                     khT[:, hb, kt * 128:(kt + 1) * 128],
                                     qhT[:, hb, qc0:qc1],
                                     start=True, stop=False)
                            return e.matmul(PS[sbk][:, cols], ones_b[:, :],
                                            shiftb[:, hb, qc0:qc1],
                                            start=False, stop=True)
                        cqk = p.op("tensor", pe_qk, waits=tuple(wts),
                                   incs=(("s_pe", 1),))
                        cmask = None
                        if d >= 0:
                            cmask = p.op("vector",
                                         lambda e, sbk=sbk, imin=imin, d=d:
                                         e.tensor_add(
                                             PS[sbk][:, imin * 128:imin * 128 + 128],
                                             PS[sbk][:, imin * 128:imin * 128 + 128],
                                             maskd_sb[:, d % 2, :]),
                                         waits=(("s_pe", cqk),
                                                ("s_c2", c_p2setup)),
                                         incs=(("s_dve", 1),))
                        wts = [("s_pe", cqk), ("s_dve", ca1)]
                        if cmask:
                            wts.append(("s_dve", cmask))
                        if pt_pv[sbk]:
                            wts.append(("s_pe", pt_pv[sbk]))
                        cexp = p.op("scalar",
                                    lambda e, sbk=sbk, cols=cols, hb=hb, kt=kt:
                                    e.activation(ptb[:, sbk, cols],
                                                 PS[sbk][:, cols], AF.Exp,
                                                 bias=alib[:, hb, kt:kt + 1],
                                                 scale=SCALE),
                                    waits=tuple(wts), incs=(("s_act", 1),))
                        st_exp[sbk] = cexp
                        exp_of[kt] = cexp
                    if j >= 1:
                        kt = kt_list[j - 1]
                        sbk = (j - 1) % 2
                        imin, ihi = rng_of[kt]
                        wts = [("s_act", exp_of[kt])]
                        if j == 1 and pv_dve:
                            wts.append(("s_dve", pv_dve))
                        fko = {i: max(0, 8 * grp + 2 * i - wt_h)
                               for i in range(4)}
                        def pe_pv(e, sbk=sbk, imin=imin, ihi=ihi, kt=kt,
                                  grp=grp, hb=hb, fko=fko):
                            for i in range(imin, ihi + 1):
                                ins = e.matmul(PS[2 + i][:, 0:129],
                                               ptb[:, sbk, i * 128:(i + 1) * 128],
                                               vhE[:, hb, kt, 0:129],
                                               start=(kt == fko[i]),
                                               stop=(kt == 8 * grp + 2 * i + 1))
                            return ins
                        cpv = p.op("tensor", pe_pv, waits=tuple(wts),
                                   incs=(("s_pe", 1),))
                        pt_pv[sbk] = cpv
                ab = (h * 4 + grp) % 2
                sat = f"s_at{ab}"
                wts = [("s_pe", pt_pv[(len(kt_list) - 1) % 2])]
                if at_st[ab]:
                    wts.append((sat, at_st[ab]))
                def dve_recip(e):
                    for i in range(4):
                        ins = e.reciprocal(rd_sb[:, i:i + 1],
                                           PS[2 + i][:, 128:129])
                    return ins
                crec = p.op("vector", dve_recip, waits=tuple(wts),
                            incs=(("s_dve", 1),))
                def dve_tail(e, ab=ab):
                    for i in range(4):
                        ins = e.tensor_scalar(atb[:, ab, i, :],
                                              PS[2 + i][:, 0:128],
                                              rd_sb[:, i:i + 1], None, ALU.mult)
                    return ins
                ctail = p.op("vector", dve_tail, waits=(("s_dve", crec),),
                             incs=(("s_dve", 1),))
                pv_dve = ctail
                if _DBG_P2 and h == 0 and grp == 1:
                    def dbg_cp(e):
                        for i in range(4):
                            ins = e.activation(dbgT["pvsb"][:, i, 0:129],
                                               PS[2 + i][:, 0:129], AF.Identity)
                        return ins
                    cdc = p.op("scalar", dbg_cp, waits=(("s_dve", ctail),),
                               incs=(("s_act", 1),))
                    p.dma("sync", dbgT["pv"].ap(), dbgT["pvsb"][:, :, :],
                          "s_dbg", waits=(("s_act", cdc),))
                    p.dma("sync", dbgT["rd"].ap(), rd_sb[:, :], "s_dbg")
                    p.dma("sync", dbgT["pt"].ap()[0, :, :], ptb[:, 0, :],
                          "s_dbg")
                    p.dma("sync", dbgT["pt"].ap()[1, :, :], ptb[:, 1, :],
                          "s_dbg")
                    cfz = p.op("vector", lambda e: e.engine_nop(),
                               waits=(("s_dbg", p.ev("s_dbg")),),
                               incs=(("s_dve", 1),))
                    pv_dve = cfz
                aout = bass.AP(T["ats"], (grp * 512) * E + h * 128,
                               [[E, 128], [128 * E, 4], [1, 128]])
                cs = p.dma("sync", aout, atb[:, ab, :, :], sat,
                           waits=(("s_dve", ctail),))
                at_st[ab] = cs
            head_pe[hb] = max(pt_pv)
            al_war[hb] = max(st_exp)
        drain_prep(1000, trw2)
    c_p2 = [("s_at0", p.ev("s_at0")), ("s_at1", p.ev("s_at1"))]
    c_pw = ("s_pw", p.ev("s_pw"))
    c_pw1 = ("s_pw1", p.ev("s_pw1"))

    ctx3 = contextlib.ExitStack()
    with ctx3:
        e_ = ctx3.enter_context
        woS = e_(nc.sbuf_tensor("woS", [128, 16, 512], BF16))
        aTg = e_(nc.sbuf_tensor("aTg", [128, 16, 512], BF16))
        gatedH = e_(nc.sbuf_tensor("gatedH", [128, 32, 512], BF16))
        xn2T = e_(nc.sbuf_tensor("xn2T", [128, 16, 512], BF16))
        SCR = e_(nc.sbuf_tensor("SCR", [128, 4, E], F32))
        x1sb = e_(nc.sbuf_tensor("x1sb", [128, E], F32))
        resid = e_(nc.sbuf_tensor("resid", [128, 2, E], F32))
        outsb = e_(nc.sbuf_tensor("outsb", [128, 2, E], F32))
        xn2b = e_(nc.sbuf_tensor("xn2b", [128, E], BF16))
        gsil = e_(nc.sbuf_tensor("gsil", [128, 4, 512], BF16))
        usb = e_(nc.sbuf_tensor("usb", [128, 4, 512], BF16))
        wgs = e_(nc.sbuf_tensor("wgs", [128, 2, 4, 512], BF16))
        wus = e_(nc.sbuf_tensor("wus", [128, 2, 4, 512], BF16))
        wds = e_(nc.sbuf_tensor("wds", [128, 2, 4, 512], BF16))

        war = dict(woS=0, aTg=0, resid=[0, 0], outsb0=0,
                   xn2b=0, wo_evac=[0, 0], wgs=[0, 0], wus=[0, 0],
                   wds=[0, 0], x1st=0, outst=0, gated_pe=0, dn_evac=0,
                   sil=0, ups=0)

        for grp in range(4):
            c_xn2T = 0
            wts = list(c_p2)
            if war["aTg"]:
                wts.append(("s_pe", war["aTg"]))
            for fc in range(16):
                p.dma("sync", aTg[:, fc, :],
                      T["ats"].ap()[grp * 512:(grp + 1) * 512,
                                    fc * 128:(fc + 1) * 128],
                      "s_aT0", waits=tuple(wts) if fc == 0 else (),
                      transpose=True)
            c_aT0 = p.ev("s_aT0")
            wo_evac_tile = [0, 0, 0, 0]
            for d2g in range(4):
                wts = [("s_aT0", c_aT0), c_pw, c_pw1]
                if war["woS"]:
                    wts.append(("s_pe", war["woS"]))
                cwo = p.dma("sync", woS[:, :, :],
                            bass.AP(T["woTd"], d2g * 512,
                                    [[E, 128], [128 * E, 16], [1, 512]]),
                            "s_wo", waits=tuple(wts))
                for ti in range(4):
                    pb = (d2g * 4 + ti) % 2
                    wts = [("s_wo", cwo), ("s_const", c_setup),
                           ("s_gi", c_rows)]
                    if war["wo_evac"][pb]:
                        wts.append(("s_act", war["wo_evac"][pb]))
                    def pe_wo(e, ti=ti, pb=pb, d2g=d2g):
                        for fc in range(16):
                            e.matmul(PS[pb][:, :],
                                     aTg[:, fc, ti * 128:(ti + 1) * 128],
                                     woS[:, fc, :],
                                     start=(fc == 0), stop=False)
                        return e.matmul(PS[pb][:, :], ones_b[:, :],
                                        borow[:, d2g * 512:(d2g + 1) * 512],
                                        start=False, stop=True)
                    cm = p.op("tensor", pe_wo, waits=tuple(wts),
                              incs=(("s_pe", 1),))
                    war["woS"] = cm
                    ce = p.op("scalar",
                              lambda e, pb=pb, ti=ti, d2g=d2g: e.activation(
                                  SCR[:, ti, d2g * 512:(d2g + 1) * 512],
                                  PS[pb][:, :], AF.Identity),
                              waits=(("s_pe", cm),), incs=(("s_act", 1),))
                    war["wo_evac"][pb] = ce
                    wo_evac_tile[ti] = ce
            war["aTg"] = war["woS"]
            for ti in range(4):
                tt = grp * 4 + ti
                rb = ti % 2
                srs = f"s_rs{rb}"
                wts = []
                if war["resid"][rb]:
                    wts.append(("s_dve", war["resid"][rb]))
                cxq = p.dma("sync", resid[:, rb, :],
                            ap("xq")[tt * 128:(tt + 1) * 128, :], srs,
                            waits=tuple(wts))
                extra = [("s_dve", war["outsb0"])] if war["outsb0"] else []
                cno = ln(SCR[:, ti, :], outsb[:, 0, :], 20 + 2 * (ti % 2),
                         (("s_act", wo_evac_tile[ti]),),
                         extra_norm_waits=extra, sq=ti % 2)
                wts = [("s_act", cno), (srs, cxq)]
                if war["x1st"]:
                    wts.append(("s_x1s", war["x1st"]))
                cx1 = p.op("vector",
                           lambda e, rb=rb: e.tensor_add(x1sb[:, :],
                                                         outsb[:, 0, :],
                                                         resid[:, rb, :]),
                           waits=tuple(wts), incs=(("s_dve", 1),))
                war["outsb0"] = cx1
                war["resid"][rb] = cx1
                cst = p.dma("sync", T["x1s"].ap()[tt * 128:(tt + 1) * 128, :],
                            x1sb[:, :], "s_x1s", waits=(("s_dve", cx1),))
                war["x1st"] = cst
                extra = [("s_pe", war["xn2b"])] if war["xn2b"] else []
                cn2 = ln(x1sb[:, :], xn2b[:, :], 21 + 2 * (ti % 2),
                         (("s_dve", cx1),),
                         extra_norm_waits=extra, sq=ti % 2)
                cpe2, cdv2 = transpose_tile(xn2b[:, :], xn2T, ti, cn2)
                war["xn2b"] = cpe2
                c_xn2T = max(c_xn2T, cdv2)
            for half in range(2):
                for hbi in range(8):
                    hbk = half * 8 + hbi
                    gld = [0, 0, 0, 0]
                    uld = [0, 0, 0, 0]
                    gmm = [0, 0, 0, 0]
                    umm = [0, 0, 0, 0]

                    def wload(dstT, dst_sb, ecq, wb, warkey, sem_pfx,
                              extra_pe=0):
                        wts = [c_pw, c_pw1]
                        if extra_pe:
                            wts.append(("s_pe", extra_pe))
                        elif war[warkey][wb]:
                            wts.append(("s_pe", war[warkey][wb]))
                        return p.dma(
                            "gpsimd", dst_sb[:, wb, :, :],
                            bass.AP(dstT, ecq * 4 * 128 * FF + hbk * 512,
                                    [[FF, 128], [128 * FF, 4], [1, 512]]),
                            f"{sem_pfx}{wb}", waits=tuple(wts))

                    gld[0] = wload(T["wgTd"], wgs, 0, 0, "wgs", "s_wg")
                    gld[1] = wload(T["wgTd"], wgs, 1, 1, "wgs", "s_wg")
                    uld[0] = wload(T["wuTd"], wus, 0, 0, "wus", "s_wu")
                    uld[1] = wload(T["wuTd"], wus, 1, 1, "wus", "s_wu")
                    for ecq in range(4):
                        wts = [(f"s_wg{ecq % 2}", gld[ecq]), ("s_dve", c_xn2T)]
                        if ecq == 0:
                            if war["dn_evac"]:
                                wts.append(("s_dve", war["dn_evac"]))
                            if war["sil"]:
                                wts.append(("s_act", war["sil"]))
                            if war["ups"]:
                                wts.append(("s_dve", war["ups"]))
                        def pe_g(e, ecq=ecq):
                            ins = None
                            for ecl in range(4):
                                ec = ecq * 4 + ecl
                                for hcl in range(4):
                                    ins = e.matmul(
                                        PS[2 + hcl][:, :],
                                        wgs[:, ecq % 2, ecl,
                                            hcl * 128:(hcl + 1) * 128],
                                        xn2T[:, ec, :],
                                        start=(ec == 0), stop=(ec == 15))
                            return ins
                        gmm[ecq] = p.op("tensor", pe_g, waits=tuple(wts),
                                        incs=(("s_pe", 1),))
                        if ecq < 2:
                            gld[ecq + 2] = wload(T["wgTd"], wgs, ecq + 2,
                                                 ecq % 2, "wgs", "s_wg",
                                                 extra_pe=gmm[ecq])
                    war["wgs"][0] = war["wgs"][1] = gmm[3]
                    def act_sil(e):
                        for hcl in range(4):
                            ins = e.activation(gsil[:, hcl, :],
                                               PS[2 + hcl][:, :], AF.Silu)
                        return ins
                    csil = p.op("scalar", act_sil, waits=(("s_pe", gmm[3]),),
                                incs=(("s_act", 1),))
                    war["sil"] = csil
                    for ecq in range(4):
                        wts = [(f"s_wu{ecq % 2}", uld[ecq])]
                        if ecq == 0:
                            wts.append(("s_act", csil))
                        def pe_u(e, ecq=ecq):
                            ins = None
                            for ecl in range(4):
                                ec = ecq * 4 + ecl
                                for hcl in range(4):
                                    ins = e.matmul(
                                        PS[2 + hcl][:, :],
                                        wus[:, ecq % 2, ecl,
                                            hcl * 128:(hcl + 1) * 128],
                                        xn2T[:, ec, :],
                                        start=(ec == 0), stop=(ec == 15))
                            return ins
                        umm[ecq] = p.op("tensor", pe_u, waits=tuple(wts),
                                        incs=(("s_pe", 1),))
                        if ecq < 2:
                            uld[ecq + 2] = wload(T["wuTd"], wus, ecq + 2,
                                                 ecq % 2, "wus", "s_wu",
                                                 extra_pe=umm[ecq])
                    war["wus"][0] = war["wus"][1] = umm[3]
                    def dve_ucp(e):
                        for hcl in range(4):
                            ins = e.tensor_copy(usb[:, hcl, :], PS[2 + hcl][:, :])
                        return ins
                    wts = [("s_pe", umm[3]), ("s_act", csil)]
                    cuc = p.op("vector", dve_ucp, waits=tuple(wts),
                               incs=(("s_dve", 1),))
                    def dve_gated(e, hbi=hbi):
                        for hcl in range(4):
                            ins = e.tensor_mul(gatedH[:, hbi * 4 + hcl, :],
                                               gsil[:, hcl, :], usb[:, hcl, :])
                        return ins
                    wts = [("s_dve", cuc)]
                    if war["gated_pe"]:
                        wts.append(("s_pe", war["gated_pe"]))
                    cgd = p.op("vector", dve_gated, waits=tuple(wts),
                               incs=(("s_dve", 1),))
                    war["ups"] = cgd
                c_gated = war["ups"]
                for d2g in range(4):
                    for hcq in range(8):
                        wb = hcq % 2
                        wts = [c_pw, c_pw1]
                        if war["wds"][wb]:
                            wts.append(("s_pe", war["wds"][wb]))
                        cwd = p.dma("gpsimd", wds[:, wb, :, :],
                                    bass.AP(T["wdTd"],
                                            (half * 32 + hcq * 4) * 128 * E
                                            + d2g * 512,
                                            [[E, 128], [128 * E, 4], [1, 512]]),
                                    f"s_wd{wb}", waits=tuple(wts))
                        wts = [(f"s_wd{wb}", cwd), ("s_dve", c_gated)]
                        if hcq == 0 and war["dn_evac"]:
                            wts.append(("s_dve", war["dn_evac"]))
                        def pe_dn(e, wb=wb, hcq=hcq):
                            ins = None
                            for hcl in range(4):
                                hc = hcq * 4 + hcl
                                for ti in range(4):
                                    ins = e.matmul(
                                        PS[2 + ti][:, :],
                                        gatedH[:, hc, ti * 128:(ti + 1) * 128],
                                        wds[:, wb, hcl, :],
                                        start=(hc == 0), stop=(hc == 31))
                            return ins
                        cdm = p.op("tensor", pe_dn, waits=tuple(wts),
                                   incs=(("s_pe", 1),))
                        war["wds"][wb] = cdm
                        if d2g == 3 and hcq == 7:
                            war["gated_pe"] = cdm
                    def dve_dn(e, d2g=d2g, half=half):
                        for ti in range(4):
                            if half == 0:
                                ins = e.tensor_copy(
                                    SCR[:, ti, d2g * 512:(d2g + 1) * 512],
                                    PS[2 + ti][:, :])
                            else:
                                ins = e.tensor_add(
                                    SCR[:, ti, d2g * 512:(d2g + 1) * 512],
                                    SCR[:, ti, d2g * 512:(d2g + 1) * 512],
                                    PS[2 + ti][:, :])
                        return ins
                    cde = p.op("vector", dve_dn,
                               waits=(("s_pe", max(war["wds"])),),
                               incs=(("s_dve", 1),))
                    war["dn_evac"] = cde
            c_ffn = war["dn_evac"]
            for ti in range(4):
                tt = grp * 4 + ti
                rb = ti % 2
                srs = f"s_rs{rb}"
                wts = [("s_x1s", war["x1st"])]
                if war["resid"][rb]:
                    wts.append(("s_dve", war["resid"][rb]))
                cx1r = p.dma("sync", resid[:, rb, :],
                             T["x1s"].ap()[tt * 128:(tt + 1) * 128, :], srs,
                             waits=tuple(wts))
                extra = [("s_dve", war["outsb0"])] if war["outsb0"] else []
                cnf = ln(SCR[:, ti, :], outsb[:, 0, :], 20 + 2 * (ti % 2),
                         (("s_dve", c_ffn),), extra_norm_waits=extra,
                         sq=ti % 2)
                wts = [("s_act", cnf), (srs, cx1r)]
                if war["outst"]:
                    wts.append(("s_out", war["outst"]))
                cfin = p.op("vector",
                            lambda e, rb=rb: e.tensor_add(outsb[:, 1, :],
                                                          outsb[:, 0, :],
                                                          resid[:, rb, :]),
                            waits=tuple(wts), incs=(("s_dve", 1),))
                war["outsb0"] = cfin
                war["resid"][rb] = cfin
                cso = p.dma("sync", ap("out")[tt * 128:(tt + 1) * 128, :],
                            outsb[:, 1, :], "s_out", waits=(("s_dve", cfin),))
                war["outst"] = cso

        final = p.ev("s_out")
        dprev = 0
        for name in _DBG_TAPS:
            t = T[name]
            o = nc.dram_tensor("dbg_" + name, list(t.shape), t.dtype,
                               kind="ExternalOutput")
            w = [("s_out", final)]
            if dprev:
                w.append(("s_dbg", dprev))
            dprev = p.dma("sync", o.ap(), t.ap(), "s_dbg", waits=tuple(w))
        p.op("sync", lambda e: None,
             waits=(("s_out", final), ("s_dbg", p.ev("s_dbg"))))


_NC_CACHE = {}


def _get_nc():
    key = ("nc", _DBG_P1, _DBG_P2) + tuple(_DBG_TAPS)
    if key not in _NC_CACHE:
        _NC_CACHE[key] = build_nc()
    return _NC_CACHE[key]


def kernel(L0_attn_x, L0_attn_Wq, L0_attn_bq, L0_attn_Wk, L0_attn_bk,
           L0_attn_Wv, L0_attn_bv, L0_attn_Wo, L0_attn_bo,
           L0_ffn_w_gate, L0_ffn_w_up, L0_ffn_w_down):
    bf = ml_dtypes.bfloat16
    x = np.ascontiguousarray(np.asarray(L0_attn_x, dtype=np.float32))
    eye = np.eye(128, dtype=bf)
    tri = np.where(np.arange(128)[:, None] > np.arange(128)[None, :],
                   np.float32(-1e30), np.float32(0.0)).astype(bf)
    fullm = np.full((128, 128), -1e30, dtype=bf)
    zerom = np.zeros((128, 128), dtype=bf)
    maskd_by_p = [np.ascontiguousarray(np.stack([tri, fullm])),
                  np.ascontiguousarray(np.stack([zerom, tri]))]

    nc = _get_nc()
    w32 = lambda a: np.ascontiguousarray(np.asarray(a, np.float32))
    weights = {"wq": w32(L0_attn_Wq), "wk": w32(L0_attn_Wk),
               "wv": w32(L0_attn_Wv), "wo": w32(L0_attn_Wo),
               "bq": w32(L0_attn_bq), "bk": w32(L0_attn_bk),
               "bv": w32(L0_attn_bv), "bo": w32(L0_attn_bo),
               "wg": w32(L0_ffn_w_gate), "wu": w32(L0_ffn_w_up),
               "wd": w32(L0_ffn_w_down)}
    in_maps = []
    for c in range(8):
        b, par = c // 2, c % 2
        xq_c = np.ascontiguousarray(
            x[b].reshape(NT, 128, E)[par::2].reshape(TOWN, E))
        posq_c = (np.arange(TOWN) % 128 + 256.0 * (np.arange(TOWN) // 128)
                  + 128.0 * par).astype(np.float32).reshape(1, TOWN)
        m = dict(weights)
        posk_c = (np.arange(128)[:, None] + 128.0 * np.arange(NT)[None, :]
                  ).astype(np.float32)
        m.update({"xb": np.ascontiguousarray(x[b]), "xq": xq_c,
                  "posq": posq_c, "posk": np.ascontiguousarray(posk_c),
                  "maskd": maskd_by_p[par], "eye": eye})
        in_maps.append(m)
    res = run_bass_kernel_spmd(nc, in_maps, core_ids=list(range(8)))
    del _LAST_RES[:]
    _LAST_RES.append(res)
    full_out = np.empty((B, S, E), dtype=np.float32)
    for c in range(8):
        b, par = c // 2, c % 2
        full_out[b].reshape(NT, 128, E)[par::2] = \
            np.asarray(res.results[c]["out"]).reshape(NQ, 128, E)
    return full_out
